# version 07
# baseline (speedup 1.0000x reference)
import numpy as np
import ml_dtypes
from contextlib import ExitStack

import concourse.bass as bass
import concourse.mybir as mybir
from concourse.bass_utils import run_bass_kernel_spmd

F32 = mybir.dt.float32
BF16 = mybir.dt.bfloat16
AF = mybir.ActivationFunctionType
ALU = mybir.AluOpType
AX = mybir.AxisListType
BFNP = ml_dtypes.bfloat16

NC = 8
D = 4096
SL = 32
TOK = 1024
NT = 64
H = 32
HD = 128
G = 8
KVD = 2048
NEXP, TOPK, RANK = 64, 16, 16
SCALING = 2.0
L1E, L1R = 4, 8
LBIG = -1e30

def own_tiles(c):
    return [c, 15 - c, 16 + c, 31 - c, 32 + c, 47 - c, 48 + c, 63 - c]

def tile_owner_slot(t):
    o = t // 8
    if o == 0: c = t
    elif o == 1: c = 15 - t
    elif o == 2: c = t - 16
    elif o == 3: c = 31 - t
    elif o == 4: c = t - 32
    elif o == 5: c = 47 - t
    elif o == 6: c = t - 48
    else: c = 63 - t
    return c, o

def _consts():
    c = {}
    c["eye"] = np.eye(128, dtype=BFNP)
    c["ones_col"] = np.ones((128, 1), dtype=BFNP)
    c["ones_row_f32"] = np.ones((1, 128), dtype=np.float32)
    b2 = np.zeros((128, 2), dtype=np.float32); b2[:, 0] = 0
    basis2 = np.zeros((128, 2), dtype=np.float32)
    basis2[:, 0] = 1.0
    c["basis2a"] = basis2.astype(BFNP)
    basis2b = np.zeros((128, 2), dtype=np.float32); basis2b[:, 1] = 1.0
    c["basis2b"] = basis2b.astype(BFNP)
    bas32 = np.zeros((128, 32 * 32), dtype=np.float32)
    for h in range(32):
        bas32[:, h * 32 + h] = 1.0
    c["basis32"] = bas32.astype(BFNP)
    L = 4096
    t64 = np.tan(np.pi * np.arange(L, dtype=np.float64) / L).astype(np.float32)
    c["t_row"] = t64.reshape(1, L)
    E = np.exp(2j * np.pi * np.outer(np.arange(64), np.arange(64)) / 64)
    TW = np.exp(2j * np.pi * np.outer(np.arange(64), np.arange(64)) / 4096)
    c["E64r"] = E.real.astype(np.float32)
    c["E64i"] = E.imag.astype(np.float32)
    c["E64i_neg"] = (-E.imag).astype(np.float32)
    c["TWr"] = TW.real.astype(np.float32)
    c["TWi"] = TW.imag.astype(np.float32)
    c["EPr"] = (E.real / L).astype(np.float32)
    c["EPi_neg"] = (-E.imag / L).astype(np.float32)
    c["eye64_f32"] = np.eye(64, dtype=np.float32)
    L = 4096
    t64 = np.tan(np.pi * np.arange(L, dtype=np.float64) / L).astype(np.float32)
    c["t_col"] = np.ascontiguousarray(t64.reshape(32, 128).T)
    c["ones_row_bf"] = np.ones((1, 128), dtype=BFNP)
    c["eye128_f32"] = np.eye(128, dtype=np.float32)
    c["ones64_row_f32"] = np.ones((1, 64), dtype=np.float32)
    return c

def _prep_inputs(inputs):
    x = np.asarray(inputs["L0_attn_x"]).reshape(8192, D)
    xT = np.ascontiguousarray(x.T)
    A0 = np.asarray(inputs["L0_ffn_A"])
    B0 = np.asarray(inputs["L0_ffn_B"])
    Wr = np.asarray(inputs["L0_ffn_Wr"])
    lA = np.asarray(inputs["L1_ffn_lora_A"])
    lB = np.asarray(inputs["L1_ffn_lora_B"])
    gw = np.asarray(inputs["L1_ffn_gate_w"])
    shared = {
        "Wq": np.ascontiguousarray(inputs["L0_attn_Wq"]),
        "Wkv": np.ascontiguousarray(inputs["L0_attn_Wkv"]),
        "Wo": np.ascontiguousarray(inputs["L0_attn_Wo"]),
        "bo": np.ascontiguousarray(np.asarray(inputs["L0_attn_bo"]).reshape(32, 128).T),
        "WrT": np.ascontiguousarray(Wr.T),
        "ATf": np.ascontiguousarray(A0.reshape(1024, D).T),
        "B0kr": np.ascontiguousarray(B0.transpose(0, 2, 1).reshape(1024, D)),
        "WupT": np.ascontiguousarray(np.asarray(inputs["L1_ffn_W_up"]).T),
        "gwT": np.ascontiguousarray(gw.T),
        "ebias": np.asarray(inputs["L1_ffn_expert_bias"]).reshape(1, 4),
        "lATf": np.ascontiguousarray(lA.reshape(32, D).T),
        "lBer": np.ascontiguousarray(lB.transpose(0, 2, 1).reshape(32, D)),
        "Lri": np.asarray(inputs["L1_attn_Lambda_ri"]),
        "pri": np.asarray(inputs["L1_attn_p_ri"]),
        "qri": np.asarray(inputs["L1_attn_q_ri"]),
        "Ctri": np.asarray(inputs["L1_attn_Ct_ri"]).reshape(64, 2),
        "Bp": np.asarray(inputs["L1_attn_Bp"]).reshape(64, 1),
        "VcrT": np.ascontiguousarray(np.asarray(inputs["L1_attn_Vc_ri"])[:, :, 0].T),
        "VciT": np.ascontiguousarray(np.asarray(inputs["L1_attn_Vc_ri"])[:, :, 1].T),
        "Dscl": np.asarray(inputs["L1_attn_D"]).reshape(1, 1),
        "lstep": np.asarray(inputs["L1_attn_log_step"]).reshape(1, 1),
    }
    tri = np.tril(np.ones((128, 128), dtype=np.float32)).T
    per_core = []
    for c in range(NC):
        own = own_tiles(c)
        cols = np.concatenate([np.arange(t * 128, (t + 1) * 128) for t in own])
        m = {"xT": np.ascontiguousarray(xT[:, cols])}
        msk = np.empty((64, 128, 128), dtype=BFNP)
        for kt in range(32):
            T = own[kt // 8]
            msk[kt] = (np.ones((128, 128), np.float32) if kt < T
                       else tri if kt == T else np.zeros((128, 128), np.float32)).astype(BFNP)
        for kt in range(32, 64):
            T = own[4 + (kt - 32) // 8]
            msk[kt] = (np.ones((128, 128), np.float32) if kt < T
                       else tri if kt == T else np.zeros((128, 128), np.float32)).astype(BFNP)
        m["masks"] = msk
        per_core.append(m)
    return shared, per_core

class PB:
    STREAMS = ("t", "v", "s", "g", "y")

    def __init__(self, nc):
        self.nc = nc
        self.streams = {k: [] for k in self.STREAMS}
        self.ctx = ExitStack()
        self.sems = {}
        self.cnt = {}

    def c(self, name):
        return self.cnt.get(name, 0)

    def sem(self, name):
        if name not in self.sems:
            self.sems[name] = self.ctx.enter_context(self.nc.semaphore(name))
            self.cnt[name] = 0
        return self.sems[name]

    def _bump(self, name, n):
        self.sem(name)
        self.cnt[name] += n
        return self.cnt[name]

    def op(self, stream, method, *args, inc=None, incn=1, **kw):
        after = None
        handle = None
        if inc is not None:
            handle = self.sem(inc)
            after = self._bump(inc, incn)
        def f(eng, method=method, args=args, kw=kw, handle=handle, incn=incn):
            ins = getattr(eng, method)(*args, **kw)
            if handle is not None:
                ins.then_inc(handle, incn)
        self.streams[stream].append(f)
        return after

    def wait(self, stream, semname, val):
        handle = self.sem(semname)
        def f(eng, handle=handle, val=val):
            eng.wait_ge(handle, val)
        self.streams[stream].append(f)

    def raw(self, stream, fn):
        self.streams[stream].append(fn)

    def v(self, method, *args, **kw):
        return self.op("v", method, *args, inc="v", **kw)

    def s(self, method, *args, **kw):
        return self.op("s", method, *args, inc="s", **kw)

    def mm(self, out, lhsT, rhs, start, stop):
        return self.op("t", "matmul", out, lhsT, rhs, start=start, stop=stop)

    def tmm(self, out, lhsT, rhs, start, stop):
        return self.op("t", "matmul", out, lhsT, rhs, start=start, stop=stop, inc="t")

    def dma(self, q, sem, out, in_):
        return self.op(q, "dma_start", inc=sem, incn=16, out=out, in_=in_)

    def build(self, name="prog"):
        for sn in sorted(self.cnt):
            if sn.startswith("q") and self.cnt[sn] > 0:
                self.wait("y", sn, self.cnt[sn])
        with self.nc.Block() as block:
            for st in self.STREAMS:
                ops = self.streams[st]
                if not ops:
                    continue
                def make(ops):
                    def body(eng):
                        for f in ops:
                            f(eng)
                    return body
                getattr(block, {"t": "tensor", "v": "vector", "s": "scalar",
                                "g": "gpsimd", "y": "sync"}[st])(make(ops))

def _load_consts(pb, nc, C, names):
    out = {}
    for n in names:
        arr = C[n]
        dt = BF16 if arr.dtype == BFNP else F32
        t = pb.ctx.enter_context(nc.sbuf_tensor("c_" + n, list(arr.shape), dt))
        h = nc.inline_tensor(np.ascontiguousarray(arr), name="i_" + n).ap()
        pb.dma("y", "qc", t[:, :] if len(arr.shape) == 2 else t[:, :, :], h)
        out[n] = t
    return out

def _wpanel_ap(W, j, rows=D, width=128):
    return W[:, j * width:(j + 1) * width].rearrange("(a p) w -> p a w", p=128)

def build_launchA(fused=False):
    nc = bass.Bass("TRN2", target_bir_lowering=False, debug=False)
    C = _consts()
    io = {}
    xT = nc.dram_tensor("xT", [D, TOK], F32, kind="ExternalInput").ap()
    Wq = nc.dram_tensor("Wq", [D, D], F32, kind="ExternalInput").ap()
    Wkv = nc.dram_tensor("Wkv", [D, KVD], F32, kind="ExternalInput").ap()
    WrT = nc.dram_tensor("WrT", [D, 64], F32, kind="ExternalInput").ap()
    ATf = nc.dram_tensor("ATf", [D, 1024], F32, kind="ExternalInput").ap()
    B0kr = nc.dram_tensor("B0kr", [1024, D], F32, kind="ExternalInput").ap()
    kvx = nc.dram_tensor("kvx", [KVD, TOK], BF16, kind="ExternalOutput").ap()
    qT = nc.dram_tensor("qT", [D, TOK], BF16, kind="ExternalOutput").ap()
    moeT = nc.dram_tensor("moeT", [D, TOK], F32, kind="ExternalOutput").ap()

    pb = PB(nc)
    CS = _load_consts(pb, nc, C, ["eye", "ones_col", "ones_row_f32"])

    sb = pb.ctx.enter_context
    XN = sb(nc.sbuf_tensor("XN", [128, SL * 1024], BF16))
    AT = sb(nc.sbuf_tensor("AT", [128, SL * 1024], BF16))
    WPAN = sb(nc.sbuf_tensor("WPAN", [128, 2 * 4096], BF16))
    MBC = sb(nc.sbuf_tensor("MBC", [128, 1024], F32))
    RBC = sb(nc.sbuf_tensor("RBC", [128, 1024], F32))
    HGT = sb(nc.sbuf_tensor("HGT", [128, 8 * 1024], BF16))
    WRTS = sb(nc.sbuf_tensor("WRTS", [128, SL * 64], BF16))
    ARENA = sb(nc.sbuf_tensor("ARENA", [128, 16384], BF16))
    AFV = ARENA[:, :].bitcast(F32)
    XSL = AFV[:, 0:2048]
    TMP = AFV[:, 2048:3072]
    STS = AFV[0:1, 3072:5120]
    ROWS = AFV[0:1, 5120:8192]
    KOUT = ARENA[:, 0:1024]
    VTT = ARENA[:, 1024:2048]
    VOUT = ARENA[:, 2048:3072]
    QOUT = ARENA[:, 3072:5120]
    SC = AFV[:, 2560:3072]
    GTMP = AFV[:, 3072:3584]
    GTMP2 = AFV[:, 3584:4096]
    MOUT = AFV[:, 4096:5120]
    GT = ARENA[:, 10240:10752]
    HGtmp = ARENA[:, 10752:11776]
    B0P = ARENA[:, 11776:13824]
    PS = sb(nc.psum_tensor("PS", [128, 3584], F32))
    PT = sb(nc.psum_tensor("PT", [128, 1024], BF16))

    emit_ln0(pb, nc, CS, xT, XN, XSL, MBC, RBC, TMP, STS, ROWS, PS, 16 * 3)
    for f in range(SL):
        pb.dma("g", "qa", AT[:, f * 1024:(f + 1) * 1024].rearrange("p (a w) -> p a w", a=1),
               ATf[f * 128:(f + 1) * 128, :].unsqueeze(1))
    at_done = pb.cnt["qa"]
    pb.dma("g", "qr", WRTS[:, :].rearrange("p (a w) -> p a w", w=64),
           WrT.rearrange("(a p) w -> p a w", p=128))
    wrt_done = pb.cnt["qr"]

    emit_kv(pb, nc, CS, Wkv, XN, WPAN, KOUT, VTT, VOUT, kvx, PS, PT)
    emit_q(pb, nc, Wq, XN, WPAN, QOUT, qT, PS)
    emit_moe(pb, nc, CS, XN, AT, at_done, WRTS, wrt_done, B0kr, B0P,
             SC, GT, GTMP, GTMP2, HGtmp, HGT, MOUT, moeT, PS, PT)

    pb.build()
    pb.ctx.close()
    return nc

def emit_ln0(pb, nc, CS, xT, XN, XSL, MBC, RBC, TMP, STS, ROWS, PS, consts_done):
    sq_cnt = [None, None]
    cast_cnt = [None] * SL
    mm_slab = [None] * SL
    sqv = TMP.bitcast(BF16)
    for i in range(SL):
        par = i % 2
        if i >= 2:
            pb.wait("y", "v", cast_cnt[i - 2])
        ld = pb.dma("y", "qx%d" % par, XSL[:, par * 1024:(par + 1) * 1024],
                    xT[i * 128:(i + 1) * 128, :])
        pb.wait("v", "qx%d" % par, ld)
        cast_cnt[i] = pb.v("tensor_copy", XN[:, i * 1024:(i + 1) * 1024],
                           XSL[:, par * 1024:(par + 1) * 1024])
        pb.wait("s", "v", cast_cnt[i])
        if i >= 2:
            pb.wait("s", "t", mm_slab[i - 2])
        sq_cnt[par] = pb.s("activation", sqv[:, par * 1024:(par + 1) * 1024],
                           XN[:, i * 1024:(i + 1) * 1024], AF.Square)
        pb.wait("t", "s", sq_cnt[par])
        if i == 0:
            pb.wait("t", "qc", consts_done)
        for hf in range(2):
            pb.mm(PS[0:1, hf * 512: (hf + 1) * 512],
                  CS["ones_col"][:, :], XN[:, i * 1024 + hf * 512: i * 1024 + (hf + 1) * 512],
                  start=(i == 0), stop=(i == SL - 1))
            mm_slab[i] = pb.tmm(PS[0:1, 1024 + hf * 512: 1024 + (hf + 1) * 512],
                                CS["ones_col"][:, :], sqv[:, par * 1024 + hf * 512: par * 1024 + (hf + 1) * 512],
                                start=(i == 0), stop=(i == SL - 1))
    stats_done = pb.cnt["t"]
    pb.wait("v", "t", stats_done)
    ev = pb.v("tensor_copy", STS[:, :], PS[0:1, 0:2048])
    m_row = ROWS[0:1, 0:1024]
    r_row = ROWS[0:1, 1024:2048]
    v_row = ROWS[0:1, 2048:3072]
    pb.wait("v", "v", ev)
    e1 = pb.v("tensor_scalar_mul", m_row, STS[0:1, 0:1024], 1.0 / D)
    e2 = pb.v("tensor_scalar_mul", v_row, STS[0:1, 1024:2048], 1.0 / D)
    pb.wait("v", "v", e1)
    c1 = pb.v("tensor_mul", r_row, m_row, m_row)
    pb.wait("v", "v", c1)
    c2 = pb.v("tensor_sub", v_row, v_row, r_row)
    pb.wait("v", "v", c2)
    c2b = pb.v("tensor_scalar_add", STS[0:1, 0:1024], v_row, 1e-5)
    pb.wait("s", "v", c2b)
    c3 = pb.s("activation", v_row, STS[0:1, 0:1024], AF.Sqrt)
    pb.wait("v", "s", c3)
    c4 = pb.v("reciprocal", r_row, v_row)
    pb.wait("t", "v", c4)
    for hf in range(2):
        pb.mm(PS[:, 2048 + hf * 512: 2048 + (hf + 1) * 512], CS["ones_row_f32"][:, :],
              m_row[:, hf * 512:(hf + 1) * 512], start=True, stop=True)
    bc1 = pb.tmm(PS[:, 3072:3584], CS["ones_row_f32"][:, :], r_row[:, 0:512],
                 start=True, stop=True)
    pb.wait("v", "t", bc1)
    pb.v("tensor_copy", MBC[:, :], PS[:, 2048:3072])
    ev2 = pb.v("tensor_copy", RBC[:, 0:512], PS[:, 3072:3584])
    pb.wait("t", "v", ev2)
    bc2 = pb.tmm(PS[:, 3072:3584], CS["ones_row_f32"][:, :], r_row[:, 512:1024],
                 start=True, stop=True)
    pb.wait("v", "t", bc2)
    pb.v("tensor_copy", RBC[:, 512:1024], PS[:, 3072:3584])
    prev_mult = None
    for i in range(SL):
        if prev_mult is not None:
            pb.wait("v", "v", prev_mult)
        t1 = pb.v("tensor_sub", TMP[:, :], XN[:, i * 1024:(i + 1) * 1024], MBC[:, :])
        pb.wait("v", "v", t1)
        prev_mult = pb.v("tensor_mul", XN[:, i * 1024:(i + 1) * 1024], TMP[:, :], RBC[:, :])
    pb.xn_done = pb.cnt["v"]

def emit_kv(pb, nc, CS, Wkv, XN, WPAN, KOUT, VTT, VOUT, kvx, PS, PT, bg_loads=()):
    xn_done = pb.xn_done
    pan_cnt = [None, None]
    mmdone = [None, None]
    evdone = [None, None]
    stdone = [None, None]
    for j in range(16):
        par = j % 2
        if j >= 2:
            pb.wait("g", "t", mmdone[par])
        pan_cnt[par] = pb.dma("g", "qw%d" % par, _sb3(WPAN, par), _wpanel_ap(Wkv, j))
        for _ in range(2):
            if bg_loads:
                bg = list(bg_loads).pop(0) if False else None
        if bg_loads:
            bg_loads = list(bg_loads)
            for _ in range(2):
                if bg_loads:
                    bg_loads.pop(0)()
        pb.wait("t", "qw%d" % par, pan_cnt[par])
        if j == 0:
            pb.wait("t", "v", xn_done)
        if j >= 2:
            pb.wait("t", "v", evdone[par])
        for hf in range(2):
            ps = PS[:, par * 1024 + hf * 512: par * 1024 + (hf + 1) * 512]
            for f in range(SL):
                last = (hf == 1) and (f == SL - 1)
                fn = pb.tmm if last else pb.mm
                r = fn(ps, WPAN[:, par * 4096 + f * 128:(par * 4096 + (f + 1) * 128)],
                       XN[:, f * 1024 + hf * 512: f * 1024 + (hf + 1) * 512],
                       start=(f == 0), stop=(f == SL - 1))
        mmdone[par] = r
        g = j // 2
        if j % 2 == 0:
            pb.wait("v", "t", mmdone[par])
            if stdone[par] is not None:
                pb.wait("v", "qk", stdone[par])
            evdone[par] = pb.v("tensor_copy", KOUT[:, :], PS[:, par * 1024: par * 1024 + 1024])
            pb.wait("y", "v", evdone[par])
            stdone[par] = pb.dma("y", "qk", kvx[g * 128:(g + 1) * 128, :], KOUT[:, :])
        else:
            pb.wait("v", "t", mmdone[par])
            evdone[par] = pb.v("tensor_copy", VTT[:, :], PS[:, par * 1024: par * 1024 + 1024])
            pb.wait("t", "v", evdone[par])
            trd = None
            for tt in range(8):
                trd = pb.op("t", "transpose", PT[:, tt * 128:(tt + 1) * 128],
                            VTT[:, tt * 128:(tt + 1) * 128], CS["eye"][:, :], inc="t")
            pb.wait("v", "t", trd)
            if stdone[par] is not None:
                pb.wait("v", "qvs", stdone[par])
            vod = pb.v("tensor_copy", VOUT[:, :], PT[:, :])
            evdone[par] = vod
            pb.wait("y", "v", vod)
            dst = kvx[1024:2048, g * 128:(g + 1) * 128].rearrange("(a p) w -> p a w", p=128)
            stdone[par] = pb.dma("y", "qvs", dst, VOUT[:, :].rearrange("p (a w) -> p a w", w=128))

def _sb3(buf, par, width=4096):
    return buf[:, par * width:(par + 1) * width].rearrange("p (a w) -> p a w", a=32)

def emit_q(pb, nc, Wq, XN, WPAN, QOUT, qT, PS):
    pan_cnt = [None, None]
    mmdone = [None, None]
    evdone = [None, None]
    stq = [None, None]
    for j in range(SL):
        par = j % 2
        pb.wait("g", "t", mmdone[par] if mmdone[par] else pb.cnt["t"])
        pan_cnt[par] = pb.dma("g", "qw%d" % par, _sb3(WPAN, par), _wpanel_ap(Wq, j))
        pb.wait("t", "qw%d" % par, pan_cnt[par])
        if evdone[par] is not None:
            pb.wait("t", "v", evdone[par])
        for hf in range(2):
            ps = PS[:, par * 1024 + hf * 512: par * 1024 + (hf + 1) * 512]
            for f in range(SL):
                last = (hf == 1) and (f == SL - 1)
                fn = pb.tmm if last else pb.mm
                r = fn(ps, WPAN[:, par * 4096 + f * 128: par * 4096 + (f + 1) * 128],
                       XN[:, f * 1024 + hf * 512: f * 1024 + (hf + 1) * 512],
                       start=(f == 0), stop=(f == SL - 1))
        mmdone[par] = r
        pb.wait("v", "t", mmdone[par])
        if stq[par] is not None:
            pb.wait("v", "qq%d" % par, stq[par])
        evdone[par] = pb.v("tensor_copy", QOUT[:, par * 1024:(par + 1) * 1024],
                           PS[:, par * 1024: par * 1024 + 1024])
        pb.wait("y", "v", evdone[par])
        stq[par] = pb.dma("y", "qq%d" % par, qT[j * 128:(j + 1) * 128, :],
                          QOUT[:, par * 1024:(par + 1) * 1024])

def emit_moe(pb, nc, CS, XN, AT, at_done, WRTS, wrt_done, B0kr, B0P,
             SC, GT, GTMP, GTMP2, HGtmp, HGT, MOUT, moeT, PS, PT):
    pb.wait("t", "v", pb.c("v"))
    gate_done = [None] * 8
    prev_ev = None
    prev_a2 = None
    for tt in range(8):
        if tt == 0:
            pb.wait("t", "qr", wrt_done)
        ps = PS[:, 0:64]
        if prev_ev is not None:
            pb.wait("t", "v", prev_ev)
        for f in range(SL):
            fn = pb.tmm if f == SL - 1 else pb.mm
            smm = fn(ps, XN[:, f * 1024 + tt * 128: f * 1024 + (tt + 1) * 128],
                     WRTS[:, f * 64:(f + 1) * 64], start=(f == 0), stop=(f == SL - 1))
        pb.wait("v", "t", smm)
        if prev_a2 is not None:
            pb.wait("v", "s", prev_a2)
        sc = SC[:, tt * 64:(tt + 1) * 64]
        prev_ev = e1 = pb.v("tensor_copy", sc, ps)
        pb.wait("v", "v", e1)
        mx = GTMP[:, 64:72]
        e2 = pb.v("max", out=mx, in_=sc)
        pb.wait("v", "v", e2)
        e3 = pb.v("match_replace", out=GTMP[:, 128:192], in_to_replace=mx,
                  in_values=sc, imm_value=LBIG)
        pb.wait("v", "v", e3)
        e4 = pb.v("max", out=GTMP[:, 72:80], in_=GTMP[:, 128:192])
        pb.wait("v", "v", e4)
        e5 = pb.v("match_replace", out=GTMP[:, 192:256], in_to_replace=GTMP[:, 72:80],
                  in_values=GTMP[:, 128:192], imm_value=LBIG)
        negm = GTMP[:, 80:81]
        nm = pb.v("tensor_scalar_mul", negm, mx[:, 0:1], -1.0)
        pb.wait("s", "v", max(e5, nm))
        a1 = pb.s("activation", GTMP2[:, 0:64], sc, AF.Exp, bias=negm)
        prev_a2 = a2 = pb.s("activation", GTMP2[:, 64:128], GTMP[:, 192:256],
                            AF.Exp, bias=negm)
        pb.wait("v", "s", a2)
        zcol = GTMP[:, 82:83]
        e6a = pb.v("tensor_sub", GTMP2[:, 128:192], GTMP2[:, 0:64], GTMP2[:, 64:128])
        pb.wait("v", "v", e6a)
        e6 = pb.v("tensor_reduce", zcol, GTMP2[:, 128:192], axis=AX.X, op=ALU.add)
        pb.wait("v", "v", e6)
        e7 = pb.v("reciprocal", GTMP[:, 84:85], zcol)
        pb.wait("v", "v", e7)
        gate_done[tt] = pb.v("tensor_scalar", GT[:, tt * 64:(tt + 1) * 64],
                             GTMP2[:, 128:192], GTMP[:, 84:85], SCALING,
                             op0=ALU.mult, op1=ALU.mult)
    pb.wait("t", "qa", at_done)
    hg_ev = [None, None]
    for tt in range(8):
        for hf in range(2):
            par = hf
            ps = PS[:, 1024 + par * 512: 1024 + (par + 1) * 512]
            if hg_ev[par] is not None:
                pb.wait("t", "v", hg_ev[par])
            for f in range(SL):
                fn = pb.tmm if f == SL - 1 else pb.mm
                hmm = fn(ps, XN[:, f * 1024 + tt * 128: f * 1024 + (tt + 1) * 128],
                         AT[:, f * 1024 + hf * 512: f * 1024 + (hf + 1) * 512],
                         start=(f == 0), stop=(f == SL - 1))
            pb.wait("v", "t", hmm)
            pb.wait("v", "v", gate_done[tt])
            gsl = GT[:, tt * 64 + hf * 32: tt * 64 + (hf + 1) * 32]
            gbc = gsl.unsqueeze(2).broadcast_to([128, 32, 16])
            hg_ev[par] = pb.v("tensor_tensor",
                              HGtmp[:, hf * 512:(hf + 1) * 512].rearrange("p (k r) -> p k r", k=32),
                              ps.rearrange("p (k r) -> p k r", k=32), gbc, op=ALU.mult)
        pb.wait("t", "v", hg_ev[1])
        if tt > 0:
            pb.wait("t", "v", pt_ev)
        trp = None
        for krc in range(8):
            trp = pb.op("t", "transpose", PT[:, krc * 128:(krc + 1) * 128],
                        HGtmp[:, krc * 128:(krc + 1) * 128], CS["eye"][:, :], inc="t")
        pb.wait("v", "t", trp)
        pt_ev = pb.v("tensor_copy",
                     HGT[:, :].rearrange("p (a w) -> p a w", w=1024)[:, :, tt * 128:(tt + 1) * 128],
                     PT[:, :].rearrange("p (a w) -> p a w", w=128))
    hgt_done = pb.cnt["v"]
    b0_cnt = [None, None]
    momm = [None, None]
    moev = [None, None]
    most = [None, None]
    for od in range(SL):
        par = od % 2
        if momm[par] is not None:
            pb.wait("g", "t", momm[par])
        b0_cnt[par] = pb.dma("g", "qb%d" % par, B0P[:, par * 1024:(par + 1) * 1024]
                             .rearrange("p (a w) -> p a w", w=128),
                             B0kr[:, od * 128:(od + 1) * 128]
                             .rearrange("(a p) w -> p a w", p=128))
        pb.wait("t", "qb%d" % par, b0_cnt[par])
        if od == 0:
            pb.wait("t", "v", hgt_done)
        if moev[par] is not None:
            pb.wait("t", "v", moev[par])
        for hf in range(2):
            ps = PS[:, 1536 + par * 1024 + hf * 512: 1536 + par * 1024 + (hf + 1) * 512]
            for krc in range(8):
                last = (hf == 1) and (krc == 7)
                fn = pb.tmm if last else pb.mm
                r = fn(ps, B0P[:, par * 1024 + krc * 128: par * 1024 + (krc + 1) * 128],
                       HGT[:, krc * 1024 + hf * 512: krc * 1024 + (hf + 1) * 512],
                       start=(krc == 0), stop=(krc == 7))
        momm[par] = r
        pb.wait("v", "t", momm[par])
        if most[0] is not None:
            pb.wait("v", "qm", most[0])
        moev[par] = pb.v("tensor_copy", MOUT[:, :],
                         PS[:, 1536 + par * 1024: 1536 + par * 1024 + 1024])
        pb.wait("y", "v", moev[par])
        most[0] = pb.dma("y", "qm", moeT[od * 128:(od + 1) * 128, :], MOUT[:, :])

_prog_cache = {}

def _get_prog(name, builder):
    if name not in _prog_cache:
        _prog_cache[name] = builder()
    return _prog_cache[name]

def kernel(**inputs):
    shared, per_core = _prep_inputs(inputs)
    cores = list(range(NC))
    nc = _get_prog("F", build_fused)
    in_maps = []
    for c in range(NC):
        m = {"xT": per_core[c]["xT"], "masks": per_core[c]["masks"]}
        for n in ("Wq", "Wkv", "WrT", "ATf", "B0kr", "Wo", "bo", "WupT", "gwT",
                  "ebias", "lATf", "lBer", "Lri", "pri", "qri", "Ctri", "Bp",
                  "VcrT", "VciT", "Dscl", "lstep"):
            m[n] = shared[n]
        in_maps.append(m)
    res = run_bass_kernel_spmd(nc, in_maps, core_ids=cores)
    out = np.empty((8192, D), dtype=np.float32)
    for c in range(NC):
        o = np.asarray(res.results[c]["outT"])
        for s, t in enumerate(own_tiles(c)):
            out[t * 128:(t + 1) * 128, :] = o[:, s * 128:(s + 1) * 128].T
    return out.reshape(1, 8192, D)

SCALE_ATT = float(1.0 / np.sqrt(128.0))

def _scol(p, kt):
    if p == 0:
        return 128 * (kt // 8)
    return 0 if kt < 32 else 128 * ((kt - 32) // 8)

def build_launchB(fused=False):
    nc = bass.Bass("TRN2", target_bir_lowering=False, debug=False)
    C = _consts()
    qT = nc.dram_tensor("qT", [D, TOK], BF16, kind="ExternalInput").ap()
    kvg = nc.dram_tensor("kvg", [8 * KVD, TOK], BF16, kind="ExternalInput").ap()
    xT = nc.dram_tensor("xT", [D, TOK], F32, kind="ExternalInput").ap()
    moeT = nc.dram_tensor("moeT", [D, TOK], F32, kind="ExternalInput").ap()
    Wo = nc.dram_tensor("Wo", [D, D], F32, kind="ExternalInput").ap()
    bo = nc.dram_tensor("bo", [128, 32], F32, kind="ExternalInput").ap()
    masks = nc.dram_tensor("masks", [64, 128, 128], BF16, kind="ExternalInput").ap()
    x1T = nc.dram_tensor("x1T", [D, TOK], F32, kind="ExternalOutput").ap()
    xn1T = nc.dram_tensor("xn1T", [D, TOK], BF16, kind="ExternalOutput").ap()
    rbounce = nc.dram_tensor("rbounce", [32, 512], F32, kind="Internal").ap()

    pb = PB(nc)
    CS = _load_consts(pb, nc, C, ["eye", "ones_col", "ones_row_f32", "basis32"])
    consts_done = 16 * 4

    sb = pb.ctx.enter_context
    QP = sb(nc.sbuf_tensor("QP", [128, 32 * 512], BF16))
    CTX = sb(nc.sbuf_tensor("CTX", [128, 32 * 512], F32))
    KB = sb(nc.sbuf_tensor("KB", [128, 2 * 4 * 1024], BF16))
    VB = sb(nc.sbuf_tensor("VB", [128, 2 * 4 * 1024], BF16))
    PSB = sb(nc.sbuf_tensor("PSB", [128, 2 * 1024], BF16))
    MKB = sb(nc.sbuf_tensor("MKB", [128, 2 * 4 * 128], BF16))
    WOP = sb(nc.sbuf_tensor("WOP", [128, 2 * 4096], BF16))
    RCP = sb(nc.sbuf_tensor("RCP", [32, 512], F32))
    RCROW = sb(nc.sbuf_tensor("RCROW", [1, 2 * 512], F32))
    RB = sb(nc.sbuf_tensor("RB", [128, 512], F32))
    BOC = sb(nc.sbuf_tensor("BOC", [128, 32], F32))
    XSL2 = sb(nc.sbuf_tensor("XSL2", [128, 2 * 512], F32))
    MSL = sb(nc.sbuf_tensor("MSL", [128, 2 * 512], F32))
    T1 = sb(nc.sbuf_tensor("T1", [128, 512], F32))
    X1O = sb(nc.sbuf_tensor("X1O", [128, 2 * 512], F32))
    X1B = sb(nc.sbuf_tensor("X1B", [128, 512], BF16))
    SQB = sb(nc.sbuf_tensor("SQB", [128, 512], BF16))
    STS2 = sb(nc.sbuf_tensor("STS2", [1, 2048], F32))
    ROW2 = sb(nc.sbuf_tensor("ROW2", [1, 3072], F32))
    MB2 = sb(nc.sbuf_tensor("MB2", [128, 1024], F32))
    RB2 = sb(nc.sbuf_tensor("RB2", [128, 1024], F32))
    XR = KB[:, :].bitcast(F32)[:, 0:2048]
    XNO = VB[:, 0:2048]
    PS = sb(nc.psum_tensor("PS", [128, 4096], F32))

    pb.dma("y", "qc", BOC[:, :], bo)
    consts_done += 16

    x1st = emit_attention(pb, nc, CS, consts_done, qT, kvg, xT, moeT, Wo, masks,
                          QP, CTX, KB, VB, PSB, MKB, WOP, RCP, RCROW, rbounce,
                          RB, BOC, XSL2, MSL, T1, X1O, X1B, SQB, STS2, PS, x1T,
                          PS[0:1, 3584:4096])
    emit_xn1(pb, nc, CS, x1T, xn1T, STS2, ROW2, MB2, RB2, XR, XNO, PS, x1st)

    pb.build()
    pb.ctx.close()
    return nc

def emit_attention(pb, nc, CS, consts_done, qT, kvg, xT, moeT, Wo, masks,
                   QP, CTX, KB, VB, PSB, MKB, WOP, RCP, RCROW, rbounce,
                   RB, BOC, XSL2, MSL, T1, X1O, X1B, SQB, STS2, PS, x1T, STP):
    BAS = CS["basis32"]
    x1_store_cnt = []
    for p in range(2):
        nkt = 32 if p == 0 else 64
        qcols = slice(p * 512, (p + 1) * 512)
        if p == 1:
            pb.wait("y", "t", pb.c("t"))
            pb.wait("t", "v", pb.c("v"))
        qld = pb.dma("y", "qqp", QP[:, :].rearrange("p (a w) -> p a w", w=512),
                     qT[:, qcols].rearrange("(a p) w -> p a w", p=128))
        nch = nkt // 4
        kv_cnt = [None, None]
        mk_cnt = [None, None]
        kv_cnt[0], mk_cnt[0] = _kv_chunk_load(pb, kvg, masks, p, 0, KB, VB, MKB)
        ctx_ev = {}
        first_sums = True
        for ch in range(nch):
            cpar = ch % 2
            if ch + 1 < nch:
                pb.wait("y", "t", pb.c("t"))
                kv_cnt[1 - cpar], mk_cnt[1 - cpar] = _kv_chunk_load(
                    pb, kvg, masks, p, ch + 1, KB, VB, MKB)
            items = [(hg, ki) for hg in range(16) for ki in range(4)]
            first_kt = 4 * ch
            scol0 = _scol(p, first_kt)

            def emit_scores(item, idx):
                hg, ki = item
                kt = 4 * ch + ki
                par = idx % 2
                sc = _scol(p, kt)
                sbank = 1024 + par * 1024
                key = par
                if idx - 2 in exp_by_idx:
                    pb.wait("t", "s", exp_by_idx[idx - 2])
                if idx == 0:
                    pb.wait("t", "qkb%d" % cpar, kv_cnt[cpar][0])
                    if mk_cnt[cpar] is not None:
                        pb.wait("t", "qmk%d" % cpar, mk_cnt[cpar])
                    if ch == 0:
                        pb.wait("t", "qqp", qld)
                        if p == 0:
                            pb.wait("t", "qc", consts_done)
                sm = None
                for hp, h in ((0, 2 * hg), (1, 2 * hg + 1)):
                    g = h // 4
                    lhsT = KB[:, cpar * 4096 + ki * 1024 + g * 128:
                              cpar * 4096 + ki * 1024 + (g + 1) * 128]
                    rhs = QP[:, h * 512 + sc:(h + 1) * 512]
                    sm = pb.tmm(PS[:, sbank + hp * 512 + sc: sbank + (hp + 1) * 512],
                                lhsT, rhs, start=True, stop=True)
                return sm

            exp_by_idx = {}
            score_by_idx = {}
            sums_by_idx = {}
            mask_by_idx = {}

            def emit_ctx_sums(jdx):
                nonlocal first_sums
                hgj, kij = items[jdx]
                ktj = 4 * ch + kij
                parj = jdx % 2
                scj = _scol(p, ktj)
                if mask_by_idx.get(jdx) is not None:
                    pb.wait("t", "v", mask_by_idx[jdx])
                else:
                    pb.wait("t", "s", exp_by_idx[jdx])
                if kij == 0 and ctx_ev.get("last") is not None:
                    pb.wait("t", "v", ctx_ev["last"])
                ctx_done = None
                for hp, h in ((0, 2 * hgj), (1, 2 * hgj + 1)):
                    g = h // 4
                    vt = VB[:, cpar * 4096 + kij * 1024 + g * 128:
                            cpar * 4096 + kij * 1024 + (g + 1) * 128]
                    prhs = PSB[:, parj * 1024 + hp * 512 + scj: parj * 1024 + (hp + 1) * 512]
                    fn = pb.tmm if (kij == 3 and hp == 1) else pb.mm
                    ctx_done = fn(PS[:, hp * 512 + scj:(hp + 1) * 512], vt, prhs,
                                  start=(kij == 0), stop=(kij == 3))
                for hp, h in ((0, 2 * hgj), (1, 2 * hgj + 1)):
                    prhs = PSB[:, parj * 1024 + hp * 512 + scj: parj * 1024 + (hp + 1) * 512]
                    last_sums = (ch == nch - 1 and hgj == 15 and kij == 3 and hp == 1)
                    sl = pb.tmm(PS[0:32, 3072 + scj:3584], BAS[:, h * 32:(h + 1) * 32],
                                prhs, start=first_sums, stop=last_sums)
                    first_sums = False
                sums_by_idx[jdx] = sl
                if kij == 3:
                    pb.wait("v", "t", ctx_done)
                    n0 = 512 - scol0
                    dst = CTX[:, 2 * hgj * 512:].rearrange(
                        "p (a w) -> p a w", w=512)[:, 0:2, scol0:512]
                    src_ = PS[:, 0:1024].rearrange(
                        "p (a w) -> p a w", w=512)[:, :, scol0:512]
                    if ch == 0:
                        ctx_ev["last"] = pb.v("tensor_copy", dst, src_)
                    else:
                        ctx_ev["last"] = pb.v("tensor_tensor", dst, dst, src_, op=ALU.add)

            score_by_idx[0] = emit_scores(items[0], 0)
            for idx, item in enumerate(items):
                hg, ki = item
                kt = 4 * ch + ki
                par = idx % 2
                sbank = 1024 + par * 1024
                pb.wait("s", "t", score_by_idx[idx])
                if idx - 2 in sums_by_idx:
                    pb.wait("s", "t", sums_by_idx[idx - 2])
                sc_i = _scol(p, kt)
                e = pb.s("activation",
                         PSB[:, par * 1024:(par + 1) * 1024].rearrange(
                             "p (a w) -> p a w", w=512)[:, :, sc_i:512],
                         PS[:, sbank: sbank + 1024].rearrange(
                             "p (a w) -> p a w", w=512)[:, :, sc_i:512],
                         AF.Exp, scale=SCALE_ATT)
                exp_by_idx[idx] = e
                if idx + 1 < len(items):
                    score_by_idx[idx + 1] = emit_scores(items[idx + 1], idx + 1)
                need_mask = (p == 0) or (kt >= 32)
                if need_mask:
                    o = (kt // 8) if p == 0 else ((kt - 32) // 8)
                    pb.wait("v", "s", e)
                    pv = PSB[:, par * 1024:(par + 1) * 1024].rearrange(
                        "p (a w) -> p a w", w=512)[:, :, o * 128:(o + 1) * 128]
                    mk = MKB[:, cpar * 512 + ki * 128:(cpar * 512 + (ki + 1) * 128)]
                    mask_by_idx[idx] = pb.v("tensor_tensor", pv, pv,
                                            mk.unsqueeze(1).broadcast_to([128, 2, 128]),
                                            op=ALU.mult)
                if idx >= 1:
                    emit_ctx_sums(idx - 1)
            emit_ctx_sums(len(items) - 1)
        pb.wait("v", "t", pb.c("t"))
        if p == 1:
            pb.wait("v", "qrb", pb.c("qrb"))
        rc = pb.v("reciprocal", RCP[:, :], PS[0:32, 3072:3584])
        pb.wait("y", "v", rc)
        b1 = pb.dma("y", "qrb", rbounce[:, :], RCP[:, :])
        pb.wait("y", "qrb", b1)
        nrm_done = _emit_normalize_ctx(pb, nc, CS, rc, CTX, QP, RB, PS, RCROW, rbounce)
        x1_store_cnt.append(
            _emit_wo_residual(pb, nc, CS, p, Wo, xT, moeT, QP, WOP, BOC,
                              XSL2, MSL, T1, X1O, X1B, SQB, STS2, PS, x1T, STP))
    return x1_store_cnt

def _kv_chunk_load(pb, kvg, masks, p, ch, KB, VB, MKB):
    cpar = ch % 2
    cnts = []
    need_mask = None
    for ki in range(4):
        kt = 4 * ch + ki
        t = kt if p == 0 else kt
        o, s = tile_owner_slot(t)
        base = 2048 * o
        kin = kvg[base: base + 1024, 128 * s: 128 * (s + 1)].rearrange(
            "(a p) w -> p a w", p=128)
        cnts.append(pb.dma("y", "qkb%d" % cpar,
                           KB[:, cpar * 4096 + ki * 1024:]
                           .rearrange("p (a w) -> p a w", w=128)[:, 0:8, :], kin))
        vin = kvg[base + 1024 + 128 * s: base + 1024 + 128 * (s + 1), :].rearrange(
            "p (a w) -> p a w", w=128)
        cnts.append(pb.dma("y", "qkb%d" % cpar,
                           VB[:, cpar * 4096 + ki * 1024:]
                           .rearrange("p (a w) -> p a w", w=128)[:, 0:8, :], vin))
    mk_cnt = None
    if p == 0 or ch >= 8:
        mrow = 4 * ch if p == 0 else 4 * ch
        mk_cnt = pb.dma("y", "qmk%d" % cpar,
                        MKB[:, cpar * 512:].rearrange("p (a w) -> p a w", w=128)[:, 0:4, :],
                        masks[mrow: mrow + 4, :, :].rearrange("a p w -> p a w"))
    return cnts[-1:] + [None], mk_cnt

def _emit_normalize_ctx(pb, nc, CS, rc, CTX, QP, RB, PS, RCROW, rbounce):
    prev_ev = None
    prev_mul = None
    rld = [None, None]
    for h in range(32):
        par = h % 2
        bank = PS[:, par * 512:(par + 1) * 512]
        if rld[par] is not None:
            pb.wait("y", "t", pb.c("t"))
        rld[par] = pb.dma("y", "qrr%d" % par, RCROW[0:1, par * 512:(par + 1) * 512],
                          rbounce[h:h + 1, :])
        if prev_ev is not None:
            pb.wait("t", "v", prev_ev)
        pb.wait("t", "qrr%d" % par, rld[par])
        bc = pb.tmm(bank, CS["ones_row_f32"][:, :], RCROW[0:1, par * 512:(par + 1) * 512], start=True, stop=True)
        pb.wait("v", "t", bc)
        if prev_mul is not None:
            pb.wait("v", "v", prev_mul)
        prev_ev = pb.v("tensor_copy", RB[:, :], bank)
        pb.wait("v", "v", prev_ev)
        prev_mul = pb.v("tensor_mul", QP[:, h * 512:(h + 1) * 512].bitcast(BF16) if False
                        else QP[:, h * 512:(h + 1) * 512],
                        CTX[:, h * 512:(h + 1) * 512], RB[:, :])
    return prev_mul

def _emit_wo_residual(pb, nc, CS, p, Wo, xT, moeT, QP, WOP, BOC,
                      XSL2, MSL, T1, X1O, X1B, SQB, STS2, PS, x1T, STP):
    pb.wait("t", "v", pb.c("v"))
    qcols = slice(p * 512, (p + 1) * 512)
    wo_cnt = [None, None]
    momm = [None, None]
    xld = [None, None]
    mld = [None, None]
    stx = [None, None]
    evd = [None, None]
    sq_prev = None
    t_entry = pb.c("t")
    for od in range(SL):
        par = od % 2
        pb.wait("g", "t", momm[par] if momm[par] is not None else t_entry)
        wo_cnt[par] = pb.dma("g", "qwo%d" % par, _sb3(WOP, par), _wpanel_ap(Wo, od))
        if xld[par] is not None:
            pb.wait("y", "v", evd[par])
        xld[par] = pb.dma("y", "qxr%d" % par, XSL2[:, par * 512:(par + 1) * 512],
                          xT[od * 128:(od + 1) * 128, qcols])
        mld[par] = pb.dma("y", "qmr%d" % par, MSL[:, par * 512:(par + 1) * 512],
                          moeT[od * 128:(od + 1) * 128, qcols])
        pb.wait("t", "qwo%d" % par, wo_cnt[par])
        if evd[par] is not None:
            pb.wait("t", "v", evd[par])
        ps = PS[:, par * 512:(par + 1) * 512]
        for f in range(SL):
            fn = pb.tmm if f == SL - 1 else pb.mm
            r = fn(ps, WOP[:, par * 4096 + f * 128: par * 4096 + (f + 1) * 128],
                   QP[:, f * 512:(f + 1) * 512], start=(f == 0), stop=(f == SL - 1))
        momm[par] = r
        pb.wait("v", "t", momm[par])
        pb.wait("v", "qxr%d" % par, xld[par])
        pb.wait("v", "qmr%d" % par, mld[par])
        if sq_prev is not None:
            pb.wait("v", "s", sq_prev)
        t1 = pb.v("scalar_tensor_tensor", T1[:, :], ps, BOC[:, od:od + 1],
                  MSL[:, par * 512:(par + 1) * 512], op0=ALU.add, op1=ALU.add)
        pb.wait("v", "v", t1)
        pb.wait("v", "qx1%d" % par,
                stx[par] if stx[par] is not None else pb.c("qx1%d" % par))
        x1 = pb.v("tensor_add", X1O[:, par * 512:(par + 1) * 512], T1[:, :],
                  XSL2[:, par * 512:(par + 1) * 512])
        pb.wait("v", "v", x1)
        evd[par] = xb = pb.v("tensor_copy", X1B[:, :], X1O[:, par * 512:(par + 1) * 512])
        pb.wait("y", "v", x1)
        stx[par] = pb.dma("y", "qx1%d" % par, x1T[od * 128:(od + 1) * 128, qcols],
                          X1O[:, par * 512:(par + 1) * 512])
        pb.wait("s", "v", xb)
        sq_prev = sq = pb.s("activation", SQB[:, :], X1B[:, :], AF.Square)
        pb.wait("t", "s", sq)
        pb.mm(STP, CS["ones_col"][:, :], X1B[:, :],
              start=(od == 0), stop=(od == SL - 1))
        st2 = pb.tmm(PS[0:1, 3072:3584], CS["ones_col"][:, :], SQB[:, :],
                     start=(od == 0), stop=(od == SL - 1))
    pb.wait("v", "t", st2)
    ev1 = pb.v("tensor_copy", STS2[0:1, p * 512:(p + 1) * 512], STP)
    ev2 = pb.v("tensor_copy", STS2[0:1, 1024 + p * 512: 1024 + (p + 1) * 512],
               PS[0:1, 3072:3584])
    pb.stats_ev = ev2
    return stx

def emit_xn1(pb, nc, CS, x1T, xn1T, STS2, ROW2, MB2, RB2, XR, XNO, PS, x1st,
             VROW=None):
    pb.wait("v", "v", pb.stats_ev)
    m_row = ROW2[0:1, 0:1024]
    r_row = ROW2[0:1, 1024:2048]
    v_row = ROW2[0:1, 2048:3072] if VROW is None else VROW
    e1 = pb.v("tensor_scalar_mul", m_row, STS2[0:1, 0:1024], 1.0 / D)
    e2 = pb.v("tensor_scalar_mul", v_row, STS2[0:1, 1024:2048], 1.0 / D)
    pb.wait("v", "v", e1)
    c1 = pb.v("tensor_mul", r_row, m_row, m_row)
    pb.wait("v", "v", max(c1, e2))
    c2 = pb.v("tensor_sub", v_row, v_row, r_row)
    pb.wait("v", "v", c2)
    c2b = pb.v("tensor_scalar_add", STS2[0:1, 0:1024], v_row, 1e-5)
    pb.wait("s", "v", c2b)
    c3 = pb.s("activation", v_row, STS2[0:1, 0:1024], AF.Sqrt)
    pb.wait("v", "s", c3)
    c4 = pb.v("reciprocal", r_row, v_row)
    pb.wait("t", "v", c4)
    pb.wait("t", "v", pb.c("v"))
    for hf in range(2):
        pb.mm(PS[:, hf * 512:(hf + 1) * 512], CS["ones_row_f32"][:, :],
              m_row[:, hf * 512:(hf + 1) * 512], start=True, stop=True)
        bc = pb.tmm(PS[:, 1024 + hf * 512: 1024 + (hf + 1) * 512],
                    CS["ones_row_f32"][:, :], r_row[:, hf * 512:(hf + 1) * 512],
                    start=True, stop=True)
    pb.wait("v", "t", bc)
    pb.v("tensor_copy", MB2[:, :], PS[:, 0:1024])
    bce = pb.v("tensor_copy", RB2[:, :], PS[:, 1024:2048])
    for sn in ("qx10", "qx11"):
        if sn in pb.cnt:
            pb.wait("y", sn, pb.cnt[sn])
    ld = [None, None]
    cast_prev = [None, None]
    sub_prev = [None, None]
    st_prev = [None, None]
    for i in range(SL):
        par = i % 2
        if sub_prev[par] is not None:
            pb.wait("y", "v", sub_prev[par])
        ld[par] = pb.dma("y", "qs1%d" % par, XR[:, par * 1024:(par + 1) * 1024],
                         x1T[i * 128:(i + 1) * 128, :])
        pb.wait("v", "qs1%d" % par, ld[par])
        if i == 0:
            pb.wait("v", "v", bce)
        sub_prev[par] = t1 = pb.v("tensor_sub", XR[:, par * 1024:(par + 1) * 1024],
                                  XR[:, par * 1024:(par + 1) * 1024], MB2[:, :])
        pb.wait("v", "v", t1)
        if st_prev[par] is not None:
            pb.wait("v", "qxn%d" % par, st_prev[par])
        mu = pb.v("tensor_mul", XNO[:, par * 1024:(par + 1) * 1024],
                  XR[:, par * 1024:(par + 1) * 1024], RB2[:, :])
        sub_prev[par] = mu
        pb.wait("y", "v", mu)
        st_prev[par] = pb.dma("y", "qxn%d" % par, xn1T[i * 128:(i + 1) * 128, :],
                              XNO[:, par * 1024:(par + 1) * 1024])

def build_launchC(fused=False):
    nc = bass.Bass("TRN2", target_bir_lowering=False, debug=False)
    C = _consts()
    xn1T = nc.dram_tensor("xn1T", [D, TOK], BF16, kind="ExternalInput").ap()
    x1T = nc.dram_tensor("x1T", [D, TOK], F32, kind="ExternalInput").ap()
    WupT = nc.dram_tensor("WupT", [D, D], F32, kind="ExternalInput").ap()
    gwT = nc.dram_tensor("gwT", [D, 4], F32, kind="ExternalInput").ap()
    ebias = nc.dram_tensor("ebias", [1, 4], F32, kind="ExternalInput").ap()
    lATf = nc.dram_tensor("lATf", [D, 32], F32, kind="ExternalInput").ap()
    lBer = nc.dram_tensor("lBer", [32, D], F32, kind="ExternalInput").ap()
    Lri = nc.dram_tensor("Lri", [64, 2], F32, kind="ExternalInput").ap()
    pri = nc.dram_tensor("pri", [64, 2], F32, kind="ExternalInput").ap()
    qri = nc.dram_tensor("qri", [64, 2], F32, kind="ExternalInput").ap()
    Ctri = nc.dram_tensor("Ctri", [64, 2], F32, kind="ExternalInput").ap()
    Bp = nc.dram_tensor("Bp", [64, 1], F32, kind="ExternalInput").ap()
    VcrT = nc.dram_tensor("VcrT", [64, 64], F32, kind="ExternalInput").ap()
    VciT = nc.dram_tensor("VciT", [64, 64], F32, kind="ExternalInput").ap()
    Dscl = nc.dram_tensor("Dscl", [1, 1], F32, kind="ExternalInput").ap()
    lstep = nc.dram_tensor("lstep", [1, 1], F32, kind="ExternalInput").ap()
    outT = nc.dram_tensor("outT", [D, TOK], F32, kind="ExternalOutput").ap()
    arv = nc.dram_tensor("arv", [2, 4096], F32, kind="Internal").ap()
    kpad = nc.dram_tensor("kpad", [4352], BF16, kind="Internal").ap()

    pb = PB(nc)
    CS = _load_consts(pb, nc, C, ["eye", "ones_col", "ones_row_f32", "t_row",
                                  "t_col", "E64r", "E64i", "E64i_neg", "TWr",
                                  "TWi", "EPr", "EPi_neg", "eye64_f32",
                                  "ones_row_bf", "ones64_row_f32", "eye128_f32"])
    consts_done = pb.c("qc")
    sb = pb.ctx.enter_context
    XN1 = sb(nc.sbuf_tensor("XN1", [128, SL * 1024], BF16))
    WUP = sb(nc.sbuf_tensor("WUP", [128, 2 * 4096], BF16))
    BAND = sb(nc.sbuf_tensor("BAND", [128, 4224], BF16))
    LBS = sb(nc.sbuf_tensor("LBS", [32, 4096], BF16))
    LAS = sb(nc.sbuf_tensor("LAS", [128, SL * 32], BF16))
    GWS = sb(nc.sbuf_tensor("GWS", [128, SL * 4], BF16))
    EBS = sb(nc.sbuf_tensor("EBS", [1, 4], BF16))
    ZGT = sb(nc.sbuf_tensor("ZGT", [32, 8 * 128], BF16))
    X1SL = sb(nc.sbuf_tensor("X1SL", [128, 2 * 512], F32))
    OUTS = sb(nc.sbuf_tensor("OUTS", [128, 2 * 512], F32))
    T1C = sb(nc.sbuf_tensor("T1C", [128, 2 * 512], F32))
    GSC = sb(nc.sbuf_tensor("GSC", [128, 16], F32))
    ZGW = sb(nc.sbuf_tensor("ZGW", [128, 32], BF16))
    D0C = sb(nc.sbuf_tensor("D0C", [128, 1], F32))
    SSC = sb(nc.sbuf_tensor("SSC", [64, 16], F32))
    V12 = sb(nc.sbuf_tensor("V12", [64, 12], F32))
    LAM = sb(nc.sbuf_tensor("LAM", [64, 8], F32))
    BPS = sb(nc.sbuf_tensor("BPS", [64, 1], F32))
    VCS = sb(nc.sbuf_tensor("VCS", [64, 128], F32))
    KLP = sb(nc.sbuf_tensor("KLP", [128, 256], F32))
    ATRI = sb(nc.sbuf_tensor("ATRI", [128, 320], F32))
    SM64 = sb(nc.sbuf_tensor("SM64", [64, 6 * 64], F32))
    XT2 = sb(nc.sbuf_tensor("XT2", [64, 64], F32))
    Z256 = sb(nc.sbuf_tensor("Z256", [1, 256], BF16))
    PS = sb(nc.psum_tensor("PS", [128, 3584], F32))
    PT = sb(nc.psum_tensor("PT", [128, 512], BF16))

    KF = XN1[:, :].bitcast(F32)
    DIB = KF[0:64, 0:4096]
    DRB = KF[0:64, 4096:8192]
    MGB = KF[0:64, 8192:12288]
    IRB = KF[0:64, 12288:16384]
    IIB = WUP[:, :].bitcast(F32)[0:64, 0:4096]

    emit_phaseK(pb, nc, CS, consts_done, Lri, pri, qri, Ctri, Bp, VcrT, VciT,
                Dscl, lstep, arv, kpad, SSC, V12, LAM, BPS, VCS, KLP, ATRI,
                SM64, XT2, Z256, D0C, DIB, DRB, MGB, IRB, IIB, PS, PT, BAND)
    emit_l1_main(pb, nc, CS, xn1T, x1T, WupT, gwT, ebias, lATf, lBer, outT,
                 XN1, WUP, BAND, LBS, LAS, GWS, EBS, ZGT, X1SL, OUTS, T1C,
                 GSC, ZGW, D0C, PS, PT)
    pb.build()
    pb.ctx.close()
    return nc

def emit_phaseK(pb, nc, CS, consts_done, Lri, pri, qri, Ctri, Bp, VcrT, VciT,
                Dscl, lstep, arv, kpad, SSC, V12, LAM, BPS, VCS, KLP, ATRI,
                SM64, XT2, Z256, D0C, DIB, DRB, MGB, IRB, IIB, PS, PT, BAND,
                trow_dram=None, TRB=None):
    ld = pb.dma("y", "qk1", LAM[:, 0:2], Lri)
    pb.dma("y", "qk1", LAM[:, 2:4], pri)
    pb.dma("y", "qk1", LAM[:, 4:6], qri)
    pb.dma("y", "qk1", LAM[:, 6:8], Ctri)
    pb.dma("y", "qk1", BPS[:, :], Bp)
    pb.dma("y", "qk1", VCS[:, 0:64], VcrT)
    pb.dma("y", "qk1", VCS[:, 64:128], VciT)
    pb.dma("y", "qk1", SSC[0:1, 0:1], lstep)
    ld = pb.dma("y", "qk1", SSC[0:1, 1:2], Dscl)
    pb.wait("s", "qk1", ld)
    pb.wait("s", "qc", consts_done)
    st = pb.s("activation", SSC[0:1, 2:3], SSC[0:1, 0:1], AF.Exp)
    pb.wait("v", "s", st)
    rc = pb.v("reciprocal", SSC[0:1, 3:4], SSC[0:1, 2:3])
    pb.wait("v", "v", rc)
    s2 = pb.v("tensor_scalar_mul", SSC[0:1, 4:5], SSC[0:1, 3:4], 2.0)
    pb.wait("t", "v", s2)
    b1 = pb.tmm(PS[0:64, 0:1], CS["ones64_row_f32"][:, :], SSC[0:1, 4:5],
                start=True, stop=True)
    b2 = pb.tmm(PS[:, 512:513], CS["ones_row_f32"][:, :], SSC[0:1, 1:2],
                start=True, stop=True)
    b3 = pb.tmm(PS[0:64, 1024:1025], VCS[:, 0:64], BPS[:, :], start=True, stop=True)
    b4 = pb.tmm(PS[0:64, 1536:1537], VCS[:, 64:128], BPS[:, :], start=True, stop=True)
    pb.wait("v", "t", b4)
    e1 = pb.v("tensor_copy", SSC[:, 5:6], PS[0:64, 0:1])
    e2 = pb.v("tensor_copy", D0C[:, :], PS[:, 512:513])
    e3 = pb.v("tensor_copy", SSC[:, 6:7], PS[0:64, 1024:1025])
    e4 = pb.v("tensor_copy", SSC[:, 7:8], PS[0:64, 1536:1537])
    pb.wait("v", "v", e4)
    Ctr, Cti = LAM[:, 6:7], LAM[:, 7:8]
    pr_, pi_ = LAM[:, 2:3], LAM[:, 3:4]
    qr_, qi_ = LAM[:, 4:5], LAM[:, 5:6]
    Bcr, Bci = SSC[:, 6:7], SSC[:, 7:8]
    def cmul(dst_r, dst_i, ar, ai, br, bi, conj_a):
        pb.wait("v", "v", pb.c("v"))
        t1 = pb.v("tensor_mul", SSC[:, 8:9], ar, br)
        t2 = pb.v("tensor_mul", SSC[:, 9:10], ai, bi)
        pb.wait("v", "v", t2)
        if conj_a:
            rr = pb.v("tensor_add", dst_r, SSC[:, 8:9], SSC[:, 9:10])
        else:
            rr = pb.v("tensor_sub", dst_r, SSC[:, 8:9], SSC[:, 9:10])
        pb.wait("v", "v", rr)
        t3 = pb.v("tensor_mul", SSC[:, 8:9], ar, bi)
        t4 = pb.v("tensor_mul", SSC[:, 9:10], ai, br)
        pb.wait("v", "v", t4)
        if conj_a:
            r = pb.v("tensor_sub", dst_i, SSC[:, 8:9], SSC[:, 9:10])
        else:
            r = pb.v("tensor_add", dst_i, SSC[:, 8:9], SSC[:, 9:10])
        return r
    cmul(V12[:, 0:1], V12[:, 4:5], Ctr, Cti, Bcr, Bci, True)
    cmul(V12[:, 1:2], V12[:, 5:6], Ctr, Cti, pr_, pi_, True)
    cmul(V12[:, 2:3], V12[:, 6:7], qr_, qi_, Bcr, Bci, True)
    vv = cmul(V12[:, 3:4], V12[:, 7:8], qr_, qi_, pr_, pi_, True)
    pb.wait("v", "v", vv)
    vneg = pb.v("tensor_scalar_mul", V12[:, 8:12], V12[:, 4:8], -1.0)
    nlr = pb.v("tensor_scalar_mul", SSC[:, 10:11], LAM[:, 0:1], -1.0)
    prev_inv = None
    prev_tb = None
    for blk in range(8):
        cols = slice(blk * 512, (blk + 1) * 512)
        if trow_dram is not None:
            if prev_tb is not None:
                pb.wait("y", "t", prev_tb)
            tl = pb.dma("y", "qtr", TRB, trow_dram[0:1, cols])
            pb.wait("t", "qtr", tl)
            trow_src = TRB
        else:
            trow_src = CS["t_row"][0:1, cols]
        pb.wait("t", "v", pb.c("v"))
        tb = pb.tmm(PS[0:64, 2048:2560], CS["ones64_row_f32"][:, :],
                    trow_src, start=True, stop=True)
        prev_tb = tb
        pb.wait("v", "t", tb)
        di = pb.v("tensor_scalar", DIB[:, cols], PS[0:64, 2048:2560],
                  SSC[:, 5:6], LAM[:, 1:2], op0=ALU.mult, op1=ALU.subtract)
        dr = pb.v("tensor_scalar", DRB[:, cols], PS[0:64, 2048:2560],
                  0.0, SSC[:, 10:11], op0=ALU.mult, op1=ALU.add)
        pb.wait("v", "v", dr)
        m1 = pb.v("tensor_mul", MGB[:, cols], DRB[:, cols], DRB[:, cols])
        m1b = pb.v("tensor_mul", IRB[:, cols], DIB[:, cols], DIB[:, cols])
        pb.wait("v", "v", max(m1, m1b))
        m2 = pb.v("tensor_add", MGB[:, cols], MGB[:, cols], IRB[:, cols])
        pb.wait("v", "v", m2)
        m2b = pb.v("tensor_scalar_min", MGB[:, cols], MGB[:, cols], 3.0e38)
        pb.wait("v", "v", m2b)
        rcb = pb.v("reciprocal", MGB[:, cols], MGB[:, cols])
        pb.wait("v", "v", rcb)
        iv1 = pb.v("tensor_mul", IRB[:, cols], DRB[:, cols], MGB[:, cols])
        iv2 = pb.v("scalar_tensor_tensor", IIB[:, cols], DIB[:, cols], -1.0,
                   MGB[:, cols], op0=ALU.mult, op1=ALU.mult)
        prev_inv = iv2
    pb.wait("t", "v", prev_inv)
    ev_prev = None
    for j in range(SL):
        cols = slice(j * 128, (j + 1) * 128)
        if ev_prev is not None:
            pb.wait("t", "v", ev_prev)
        pb.mm(PS[:, 3072:3076], IRB[:, cols], V12[:, 0:4], start=True, stop=False)
        pb.mm(PS[:, 3072:3076], IIB[:, cols], V12[:, 8:12], start=False, stop=True)
        pb.mm(PS[:, 3076:3080], IIB[:, cols], V12[:, 0:4], start=True, stop=False)
        mmj = pb.tmm(PS[:, 3076:3080], IRB[:, cols], V12[:, 4:8], start=False, stop=True)
        pb.wait("v", "t", mmj)
        ev_prev = pb.v("tensor_copy",
                       KLP[:, :].rearrange("p (c a) -> p c a", a=32)[:, :, j:j + 1],
                       PS[:, 3072:3080].unsqueeze(2))
    def vs(method, *a, **kw):
        pb.wait("v", "v", pb.c("v"))
        return pb.v(method, *a, **kw)
    pb.wait("v", "v", ev_prev)
    k = lambda c0: KLP[:, c0 * 32:(c0 + 1) * 32]
    k00r, k01r, k10r, k11r = k(0), k(1), k(2), k(3)
    k00i, k01i, k10i, k11i = k(4), k(5), k(6), k(7)
    AR = lambda c0: ATRI[:, c0 * 32:(c0 + 1) * 32]
    denr, deni, magd = AR(0), AR(1), AR(2)
    qr2, qi2, s1_, s2_ = AR(3), AR(4), AR(5), AR(6)
    atr_r, atr_i = AR(7), AR(8)
    e = vs("tensor_scalar_add", denr, k11r, 1.0)
    vs("tensor_copy", deni, k11i)
    pb.wait("v", "v", pb.c("v"))
    e = vs("tensor_mul", magd, denr, denr)
    e = vs("tensor_mul", s1_, deni, deni)
    e = vs("tensor_add", magd, magd, s1_)
    e = vs("tensor_scalar_min", magd, magd, 3.0e38)
    e = vs("reciprocal", magd, magd)
    e1 = vs("tensor_mul", s1_, k01r, k10r)
    e2 = vs("tensor_mul", s2_, k01i, k10i)
    pb.wait("v", "v", e2)
    e3 = vs("tensor_sub", qr2, s1_, s2_)
    e1 = vs("tensor_mul", s1_, k01r, k10i)
    e2 = vs("tensor_mul", s2_, k01i, k10r)
    pb.wait("v", "v", e2)
    e4 = vs("tensor_add", qi2, s1_, s2_)
    pb.wait("v", "v", e4)
    e1 = vs("tensor_mul", s1_, qr2, denr)
    e2 = vs("tensor_mul", s2_, qi2, deni)
    pb.wait("v", "v", e2)
    e3 = vs("tensor_add", s1_, s1_, s2_)
    pb.wait("v", "v", e3)
    e3 = vs("tensor_mul", s1_, s1_, magd)
    e5 = vs("tensor_mul", s2_, qi2, denr)
    pb.wait("v", "v", e5)
    e6 = vs("scalar_tensor_tensor", s2_, qr2, -1.0, deni, op0=ALU.mult, op1=ALU.mult)
    pb.wait("v", "v", pb.c("v"))
    e7 = vs("tensor_mul", qr2, qr2, deni)
    e8 = vs("tensor_mul", qi2, qi2, denr)
    pb.wait("v", "v", max(e7, e8))
    e9 = vs("tensor_sub", s2_, qi2, qr2)
    pb.wait("v", "v", e9)
    e9 = vs("tensor_mul", s2_, s2_, magd)
    pb.wait("v", "v", e9)
    i1 = vs("tensor_sub", denr, k00r, s1_)
    i2 = vs("tensor_sub", deni, k00i, s2_)
    pb.wait("v", "v", max(i1, i2))
    tc = CS["t_col"]
    a1 = vs("tensor_mul", s1_, tc[:, :], deni)
    a2 = vs("tensor_mul", s2_, tc[:, :], denr)
    pb.wait("v", "v", max(a1, a2))
    a3 = vs("tensor_sub", atr_r, denr, s1_)
    a4 = vs("tensor_add", atr_i, deni, s2_)
    pb.wait("t", "v", max(a3, a4))
    tr1 = pb.op("t", "transpose", PS[0:32, 3072:3200], atr_r,
                CS["eye128_f32"][:, :], inc="t")
    tr2 = pb.op("t", "transpose", PS[0:32, 3200:3328], atr_i,
                CS["eye128_f32"][:, :], inc="t")
    pb.wait("v", "t", tr2)
    evt = pb.v("tensor_copy", ATRI[0:32, 0:256], PS[0:32, 3072:3328])
    pb.wait("y", "v", evt)
    s1d = pb.dma("y", "qk2", arv[0:1, :].rearrange("o (a w) -> o a w", w=128),
                 ATRI[0:32, 0:128].unsqueeze(1))
    s2d = pb.dma("y", "qk2", arv[1:2, :].rearrange("o (a w) -> o a w", w=128),
                 ATRI[0:32, 128:256].unsqueeze(1))
    pb.wait("y", "qk2", s2d)
    l1 = pb.dma("y", "qk2", SM64[:, 0:64], arv[0:1, :].rearrange("o (a w) -> (o a) w", w=64))
    l2 = pb.dma("y", "qk2", SM64[:, 64:128], arv[1:2, :].rearrange("o (a w) -> (o a) w", w=64))
    pb.wait("t", "qk2", l2)
    ARE, AIM = SM64[:, 0:64], SM64[:, 64:128]
    pb.mm(PS[0:64, 3072:3136], ARE, CS["E64r"][:, :], start=True, stop=False)
    g1 = pb.tmm(PS[0:64, 3072:3136], AIM, CS["E64i_neg"][:, :], start=False, stop=True)
    pb.mm(PS[0:64, 3136:3200], ARE, CS["E64i"][:, :], start=True, stop=False)
    g2 = pb.tmm(PS[0:64, 3136:3200], AIM, CS["E64r"][:, :], start=False, stop=True)
    pb.wait("v", "t", g2)
    gv = pb.v("tensor_copy", SM64[0:64, 128:256], PS[0:64, 3072:3200])
    pb.wait("v", "v", gv)
    GRE, GIM = SM64[:, 128:192], SM64[:, 192:256]
    HRE, HIM = SM64[:, 256:320], SM64[:, 320:384]
    t1 = pb.v("tensor_mul", HRE, GRE, CS["TWr"][:, :])
    t2 = pb.v("tensor_mul", XT2[:, :], GIM, CS["TWi"][:, :])
    pb.wait("v", "v", max(t1, t2))
    t3 = pb.v("tensor_sub", HRE, HRE, XT2[:, :])
    t4 = pb.v("tensor_mul", HIM, GIM, CS["TWr"][:, :])
    pb.wait("v", "v", max(t3, t4))
    t6 = pb.v("tensor_mul", XT2[:, :], GRE, CS["TWi"][:, :])
    pb.wait("v", "v", t6)
    t7 = pb.v("tensor_add", HIM, HIM, XT2[:, :])
    pb.wait("t", "v", t7)
    pb.mm(PS[0:64, 3072:3136], HRE, CS["EPr"][:, :], start=True, stop=False)
    x2 = pb.tmm(PS[0:64, 3072:3136], HIM, CS["EPi_neg"][:, :], start=False, stop=True)
    pb.wait("v", "t", x2)
    xv = pb.v("tensor_copy", XT2[:, :], PS[0:64, 3072:3136])
    pb.wait("t", "v", xv)
    xt = pb.op("t", "transpose", PS[0:64, 3136:3200], XT2[:, :],
               CS["eye64_f32"][0:64, :], inc="t")
    pb.wait("v", "t", xt)
    xv2 = pb.v("tensor_copy", XT2[:, :], PS[0:64, 3136:3200])
    zm = pb.v("memset", Z256[:, :], 0.0)
    pb.wait("g", "v", max(xv2, zm))
    z1 = pb.dma("g", "qk3", kpad[0:256].unsqueeze(0), Z256[:, :])
    k1_ = pb.dma("g", "qk3", kpad[256:4352].rearrange("(a w) -> a w", w=64), XT2[:, :])
    pb.wait("g", "qk3", k1_)
    pb.wait("y", "qk3", k1_)
    pb.wait("s", "qk3", k1_)
    for p128 in range(128):
        q = ("y", "s")[p128 % 2]
        pb.dma(q, "qbd" if q == "y" else "qbd2", BAND[p128:p128 + 1, :],
               kpad[128 - p128: 128 - p128 + 4224].unsqueeze(0))
    pb.kband_done = pb.c("qbd")

def emit_l1_main(pb, nc, CS, xn1T, x1T, WupT, gwT, ebias, lATf, lBer, outT,
                 XN1, WUP, BAND, LBS, LAS, GWS, EBS, ZGT, X1SL, OUTS, T1C,
                 GSC, ZGW, D0C, PS, PT):
    pb.wait("y", "t", pb.c("t"))
    pb.wait("y", "v", pb.c("v"))
    for sn in ("qxn0", "qxn1"):
        if sn in pb.cnt and pb.cnt[sn] > 0:
            pb.wait("y", sn, pb.cnt[sn])
    xl = pb.dma("y", "qxn1r", XN1[:, :].rearrange("p (a w) -> p a w", w=1024),
                xn1T.rearrange("(a p) w -> p a w", p=128))
    pb.wait("g", "t", pb.c("t"))
    pb.wait("g", "v", pb.c("v"))
    ll = pb.dma("g", "qlw", LAS[:, :].rearrange("p (a w) -> p a w", w=32),
                lATf.rearrange("(a p) w -> p a w", p=128))
    gl = pb.dma("g", "qlw", GWS[:, :].rearrange("p (a w) -> p a w", w=4),
                gwT.rearrange("(a p) w -> p a w", p=128))
    el = pb.dma("g", "qlw", EBS[:, :].unsqueeze(1), ebias.unsqueeze(1))
    bl = pb.dma("g", "qlw", LBS[:, :].rearrange("p (a w) -> p a w", w=128),
                lBer.rearrange("p (a w) -> p a w", w=128))
    prev_zg = None
    prev_tr = None
    for tt in range(8):
        pb.wait("t", "qxn1r", xl)
        pb.wait("t", "qlw", bl)
        if prev_zg is not None:
            pb.wait("t", "v", prev_zg)
        for f in range(SL):
            pb.mm(PS[:, 0:4], XN1[:, f * 1024 + tt * 128: f * 1024 + (tt + 1) * 128],
                  GWS[:, f * 4:(f + 1) * 4], start=(f == 0), stop=False)
        lmm = pb.tmm(PS[:, 0:4], CS["ones_row_bf"][:, :], EBS[:, :],
                     start=False, stop=True)
        for f in range(SL):
            fn = pb.tmm if f == SL - 1 else pb.mm
            zmm = fn(PS[:, 512:544], XN1[:, f * 1024 + tt * 128: f * 1024 + (tt + 1) * 128],
                     LAS[:, f * 32:(f + 1) * 32], start=(f == 0), stop=(f == SL - 1))
        pb.wait("v", "t", zmm)
        lg = GSC[:, 0:4]
        ev = pb.v("tensor_copy", lg, PS[:, 0:4])
        pb.wait("v", "v", ev)
        mx = pb.v("tensor_reduce", GSC[:, 4:5], lg, axis=AX.X, op=ALU.max)
        pb.wait("v", "v", mx)
        ge = pb.v("tensor_single_scalar", GSC[:, 5:9], lg, GSC[:, 4:5], op=ALU.is_ge)
        pb.wait("v", "v", ge)
        oh = GSC[:, 12:16]
        e1 = pb.v("tensor_copy", oh[:, 0:1], GSC[:, 5:6])
        n0 = pb.v("tensor_scalar", GSC[:, 9:10], GSC[:, 5:6], -1.0, 1.0,
                  op0=ALU.mult, op1=ALU.add)
        pb.wait("v", "v", n0)
        g1 = pb.v("tensor_mul", oh[:, 1:2], GSC[:, 6:7], GSC[:, 9:10])
        pb.wait("v", "v", g1)
        t1 = pb.v("tensor_scalar", GSC[:, 10:11], oh[:, 1:2], -1.0, 1.0,
                  op0=ALU.mult, op1=ALU.add)
        pb.wait("v", "v", t1)
        n1 = pb.v("tensor_mul", GSC[:, 10:11], GSC[:, 9:10], GSC[:, 10:11])
        pb.wait("v", "v", n1)
        g2 = pb.v("tensor_mul", oh[:, 2:3], GSC[:, 7:8], GSC[:, 10:11])
        pb.wait("v", "v", g2)
        t2 = pb.v("tensor_scalar", GSC[:, 11:12], oh[:, 2:3], -1.0, 1.0,
                  op0=ALU.mult, op1=ALU.add)
        pb.wait("v", "v", t2)
        n2 = pb.v("tensor_mul", GSC[:, 11:12], GSC[:, 10:11], GSC[:, 11:12])
        pb.wait("v", "v", n2)
        g3 = pb.v("tensor_mul", oh[:, 3:4], GSC[:, 8:9], GSC[:, 11:12])
        pb.wait("v", "v", g3)
        prev_zg = zg = pb.v("tensor_tensor",
                            ZGW[:, :].rearrange("p (e r) -> p e r", e=4),
                            PS[:, 512:544].rearrange("p (e r) -> p e r", e=4),
                            oh.unsqueeze(2).broadcast_to([128, 4, 8]), op=ALU.mult)
        pb.wait("t", "v", zg)
        if prev_tr is not None:
            pb.wait("t", "v", prev_tr)
        trz = pb.op("t", "transpose", PT[0:32, 0:128], ZGW[:, :], CS["eye"][:, :], inc="t")
        pb.wait("v", "t", trz)
        prev_tr = pb.v("tensor_copy", ZGT[:, tt * 128:(tt + 1) * 128], PT[0:32, 0:128])
    zgt_done = prev_tr
    wu_cnt = [None, None]
    momm = [None, None]
    evd = [None, None]
    std = [None, None]
    xld = [None, None]
    t_entry = pb.c("t")
    for od in range(SL):
        par = od % 2
        pb.wait("g", "t", momm[par] if momm[par] is not None else t_entry)
        wu_cnt[par] = pb.dma("g", "qwu%d" % par, _sb3(WUP, par), _wpanel_ap(WupT, od))
        if xld[par] is not None:
            pb.wait("y", "v", evd[par])
        xld[par] = pb.dma("y", "qx1r%d" % par,
                          X1SL[:, par * 512:(par + 1) * 512].unsqueeze(1),
                          x1T[od * 128:(od + 1) * 128, 0:512].unsqueeze(1))
        xl2 = pb.dma("y", "qx1r%d" % par,
                     T1C[:, par * 512:(par + 1) * 512].unsqueeze(1),
                     x1T[od * 128:(od + 1) * 128, 512:1024].unsqueeze(1))
        pb.wait("t", "qwu%d" % par, wu_cnt[par])
        pb.wait("t", "qbd", pb.c("qbd"))
        pb.wait("t", "qbd2", pb.c("qbd2"))
        if od == 0:
            pb.wait("t", "v", zgt_done)
        if evd[par] is not None:
            pb.wait("t", "v", evd[par])
        for hf in range(2):
            ps = PS[:, 1024 + par * 1024 + hf * 512: 1024 + par * 1024 + (hf + 1) * 512]
            for f in range(SL):
                pb.mm(ps, WUP[:, par * 4096 + f * 128: par * 4096 + (f + 1) * 128],
                      XN1[:, f * 1024 + hf * 512: f * 1024 + (hf + 1) * 512],
                      start=(f == 0), stop=False)
            for mb in range(od + 1):
                pb.mm(ps, BAND[:, (od - mb + 1) * 128: (od - mb + 2) * 128],
                      XN1[:, mb * 1024 + hf * 512: mb * 1024 + (hf + 1) * 512],
                      start=False, stop=False)
            r = pb.tmm(ps, LBS[:, od * 128:(od + 1) * 128],
                       ZGT[0:32, hf * 512:(hf + 1) * 512],
                       start=False, stop=True)
        momm[par] = r
        pb.wait("v", "t", momm[par])
        pb.wait("v", "qx1r%d" % par, xl2)
        if std[par] is not None:
            pb.wait("v", "qo%d" % par, std[par])
        ee = []
        for hf in range(2):
            ps = PS[:, 1024 + par * 1024 + hf * 512: 1024 + par * 1024 + (hf + 1) * 512]
            xn1sl = XN1[:, od * 1024 + hf * 512: od * 1024 + (hf + 1) * 512]
            a = pb.v("scalar_tensor_tensor", OUTS[:, par * 512:(par + 1) * 512],
                     xn1sl, D0C[:, :], ps, op0=ALU.mult, op1=ALU.add)
            pb.wait("v", "v", a)
            x1s = X1SL[:, par * 512:(par + 1) * 512] if hf == 0 else T1C[:, par * 512:(par + 1) * 512]
            b = pb.v("tensor_add", OUTS[:, par * 512:(par + 1) * 512],
                     OUTS[:, par * 512:(par + 1) * 512], x1s)
            pb.wait("y", "v", b)
            ee.append(b)
            std[par] = pb.dma("y", "qo%d" % par,
                              outT[od * 128:(od + 1) * 128, hf * 512:(hf + 1) * 512],
                              OUTS[:, par * 512:(par + 1) * 512])
            if hf == 0:
                pb.wait("v", "qo%d" % par, std[par])
        evd[par] = ee[-1]

def _phase_barrier(pb):
    names = list(pb.cnt.keys())
    for st in ("t", "v", "s", "g", "y"):
        for n in names:
            if pb.cnt[n] > 0:
                pb.wait(st, n, pb.cnt[n])

def build_fused(sim_local_gather=False):
    nc = bass.Bass("TRN2", target_bir_lowering=False, debug=False)
    C = _consts()
    xT = nc.dram_tensor("xT", [D, TOK], F32, kind="ExternalInput").ap()
    Wq = nc.dram_tensor("Wq", [D, D], F32, kind="ExternalInput").ap()
    Wkv = nc.dram_tensor("Wkv", [D, KVD], F32, kind="ExternalInput").ap()
    WrT = nc.dram_tensor("WrT", [D, 64], F32, kind="ExternalInput").ap()
    ATf = nc.dram_tensor("ATf", [D, 1024], F32, kind="ExternalInput").ap()
    B0kr = nc.dram_tensor("B0kr", [1024, D], F32, kind="ExternalInput").ap()
    Wo = nc.dram_tensor("Wo", [D, D], F32, kind="ExternalInput").ap()
    bo = nc.dram_tensor("bo", [128, 32], F32, kind="ExternalInput").ap()
    masks = nc.dram_tensor("masks", [64, 128, 128], BF16, kind="ExternalInput").ap()
    WupT = nc.dram_tensor("WupT", [D, D], F32, kind="ExternalInput").ap()
    gwT = nc.dram_tensor("gwT", [D, 4], F32, kind="ExternalInput").ap()
    ebias = nc.dram_tensor("ebias", [1, 4], F32, kind="ExternalInput").ap()
    lATf = nc.dram_tensor("lATf", [D, 32], F32, kind="ExternalInput").ap()
    lBer = nc.dram_tensor("lBer", [32, D], F32, kind="ExternalInput").ap()
    Lri = nc.dram_tensor("Lri", [64, 2], F32, kind="ExternalInput").ap()
    pri = nc.dram_tensor("pri", [64, 2], F32, kind="ExternalInput").ap()
    qri = nc.dram_tensor("qri", [64, 2], F32, kind="ExternalInput").ap()
    Ctri = nc.dram_tensor("Ctri", [64, 2], F32, kind="ExternalInput").ap()
    Bp = nc.dram_tensor("Bp", [64, 1], F32, kind="ExternalInput").ap()
    VcrT = nc.dram_tensor("VcrT", [64, 64], F32, kind="ExternalInput").ap()
    VciT = nc.dram_tensor("VciT", [64, 64], F32, kind="ExternalInput").ap()
    Dscl = nc.dram_tensor("Dscl", [1, 1], F32, kind="ExternalInput").ap()
    lstep = nc.dram_tensor("lstep", [1, 1], F32, kind="ExternalInput").ap()
    outT = nc.dram_tensor("outT", [D, TOK], F32, kind="ExternalOutput").ap()
    kvx = nc.dram_tensor("kvx", [KVD, TOK], BF16).ap()
    kvg = nc.dram_tensor("kvg", [8 * KVD, TOK], BF16).ap()
    qTd = nc.dram_tensor("qTd", [D, TOK], BF16).ap()
    moeTd = nc.dram_tensor("moeTd", [D, TOK], F32).ap()
    x1Td = nc.dram_tensor("x1Td", [D, TOK], F32).ap()
    xn1Td = nc.dram_tensor("xn1Td", [D, TOK], BF16).ap()
    rbounce = nc.dram_tensor("rbounce", [32, 512], F32).ap()
    arv = nc.dram_tensor("arv", [2, 4096], F32).ap()
    kpad = nc.dram_tensor("kpad", [4352], BF16).ap()

    pb = PB(nc)
    CS = _load_consts(pb, nc, C, ["eye", "ones_col", "ones_row_f32", "basis32",
                                  "t_col", "E64r", "E64i", "E64i_neg",
                                  "TWr", "TWi", "EPr", "EPi_neg", "eye64_f32",
                                  "ones_row_bf", "ones64_row_f32", "eye128_f32"])
    trow_dram = nc.inline_tensor(np.ascontiguousarray(C["t_row"]), name="i_t_row").ap()
    sb = pb.ctx.enter_context
    BOC = sb(nc.sbuf_tensor("BOC", [128, 32], F32))
    pb.dma("y", "qc", BOC[:, :], bo)
    consts_done = pb.c("qc")

    BIG1 = sb(nc.sbuf_tensor("BIG1", [128, 32768], BF16))
    BIG2 = sb(nc.sbuf_tensor("BIG2", [128, 32768], BF16))
    BIG3 = sb(nc.sbuf_tensor("BIG3", [128, 22016], BF16))
    WPAN = sb(nc.sbuf_tensor("WPAN", [128, 2 * 4096], BF16))
    MBC = sb(nc.sbuf_tensor("MBC", [128, 1024], F32))
    RBC = sb(nc.sbuf_tensor("RBC", [128, 1024], F32))
    WRTS = sb(nc.sbuf_tensor("WRTS", [128, SL * 64], BF16))
    PS = sb(nc.psum_tensor("PS", [128, 3584], F32))
    PT = sb(nc.psum_tensor("PT", [128, 1024], BF16))
    PTF = PT[:, :].bitcast(F32)

    B3F = BIG3[:, :].bitcast(F32)

    XN = BIG1[:, :]
    AT = BIG2[:, :]
    XSL = B3F[:, 0:2048]
    TMP = B3F[:, 2048:3072]
    STS = B3F[0:1, 3072:5120]
    ROWS = B3F[0:1, 5120:8192]
    KOUT = BIG3[:, 0:1024]
    VTT = BIG3[:, 1024:2048]
    VOUT = BIG3[:, 2048:3072]
    QOUT = BIG3[:, 3072:5120]
    SC = B3F[:, 2560:3072]
    GTMP = B3F[:, 3072:3584]
    GTMP2 = B3F[:, 3584:4096]
    MOUT = B3F[:, 4096:5120]
    GT = BIG3[:, 10240:10752]
    HGtmp = BIG3[:, 10752:11776]
    B0P = BIG3[:, 11776:13824]
    HGT = BIG3[:, 13824:22016]

    emit_ln0(pb, nc, CS, xT, XN, XSL, MBC, RBC, TMP, STS, ROWS, PS, consts_done)
    pb.dma("g", "qr", WRTS[:, :].rearrange("p (a w) -> p a w", w=64),
           WrT.rearrange("(a p) w -> p a w", p=128))
    wrt_done = pb.cnt["qr"]
    def at_load(f):
        pb.dma("g", "qa", AT[:, f * 1024:(f + 1) * 1024].rearrange("p (a w) -> p a w", a=1),
               ATf[f * 128:(f + 1) * 128, :].unsqueeze(1))
    emit_kv(pb, nc, CS, Wkv, XN, WPAN, KOUT, VTT, VOUT, kvx, PS, PT,
            bg_loads=[(lambda f=f: at_load(f)) for f in range(SL)])
    at_done = pb.cnt["qa"]
    pb.wait("g", "qk", pb.c("qk"))
    pb.wait("g", "qvs", pb.c("qvs"))
    if sim_local_gather:
        for rr in range(NC):
            pb.dma("g", "qcc", kvg[rr * KVD:(rr + 1) * KVD, :], kvx[:, :])
        pb.op("g", "nop", inc="cc")
    else:
        pb.op("g", "collective_compute", "AllGather", ALU.bypass,
              replica_groups=[list(range(NC))], ins=[kvx.opt()], outs=[kvg.opt()],
              inc="cc")
    emit_q(pb, nc, Wq, XN, WPAN, QOUT, qTd, PS)
    emit_moe(pb, nc, CS, XN, AT, at_done, WRTS, wrt_done, B0kr, B0P,
             SC, GT, GTMP, GTMP2, HGtmp, HGT, MOUT, moeTd, PS, PT)

    _phase_barrier(pb)
    QP = BIG2[:, 0:16384]
    CTX = BIG1[:, :].bitcast(F32)
    KB = BIG2[:, 16384:24576]
    VB = BIG2[:, 24576:32768]
    WOP = WPAN[:, :]
    PSB = BIG3[:, 0:2048]
    MKB = BIG3[:, 2048:3072]
    RCP = B3F[0:32, 1536:2048]
    RB = B3F[:, 2048:2560]
    XSL2 = B3F[:, 2560:3584]
    MSL = B3F[:, 3584:4608]
    T1 = B3F[:, 4608:5120]
    X1O = B3F[:, 5120:6144]
    X1B = BIG3[:, 12288:12800]
    SQB = BIG3[:, 12800:13312]
    STS2 = B3F[0:1, 6656:8704]
    ROW2 = B3F[0:1, 8704:10752]
    MB2 = MBC[:, :]
    RB2 = RBC[:, :]
    RCROW = WRTS[:, :].bitcast(F32)[0:1, 0:1024]

    pb.wait("y", "cc", 1)
    x1st = emit_attention(pb, nc, CS, consts_done, qTd, kvg, xT, moeTd, Wo, masks,
                          QP, CTX, KB, VB, PSB, MKB, WOP, RCP, RCROW, rbounce,
                          RB, BOC, XSL2, MSL, T1, X1O, X1B, SQB, STS2, PS, x1Td,
                          PTF[0:1, 0:512])
    XR = BIG2[:, 18724:22820].bitcast(F32)
    XNO = BIG2[:, 22820:24868]
    emit_xn1(pb, nc, CS, x1Td, xn1Td, STS2, ROW2, MB2, RB2, XR, XNO, PS, x1st,
             VROW=MBC[0:1, 0:1024])

    _phase_barrier(pb)
    XN1 = BIG1[:, :]
    KF = BIG1[:, :].bitcast(F32)
    DIB = KF[0:64, 0:4096]
    DRB = KF[0:64, 4096:8192]
    MGB = KF[0:64, 8192:12288]
    IRB = KF[0:64, 12288:16384]
    IIB = BIG2[:, :].bitcast(F32)[0:64, 0:4096]
    WUP = BIG2[:, 0:8192]
    BAND = BIG2[:, 8192:12416]
    LBS = BIG2[0:32, 12416:16512]
    LAS = BIG2[:, 16512:17536]
    GWS = BIG2[:, 17536:17664]
    EBS = BIG2[0:1, 17664:17668]
    ZGT = BIG2[0:32, 17668:18692]
    ZGW = BIG2[:, 18692:18724]
    X1SL = BIG2[:, 20480:22528].bitcast(F32)
    OUTS = BIG2[:, 22528:24576].bitcast(F32)
    T1C = BIG2[:, 24576:26624].bitcast(F32)
    B2F = BIG2[:, :].bitcast(F32)
    GSC = B2F[:, 13312:13328]
    D0C = B2F[:, 13328:13329]
    SSC = B2F[0:64, 13330:13346]
    V12 = B2F[0:64, 13346:13358]
    LAM = B2F[0:64, 13358:13366]
    BPS = B2F[0:64, 13366:13367]
    VCS = B2F[0:64, 13368:13496]
    KLP = B2F[:, 13496:13752]
    ATRI = B2F[:, 13752:14072]
    SM64 = B2F[0:64, 14072:14456]
    XT2 = B2F[0:64, 14456:14520]
    TRB = B2F[0:1, 14520:15032]
    Z256 = BIG2[0:1, 30720:30976]
    emit_phaseK(pb, nc, CS, consts_done, Lri, pri, qri, Ctri, Bp, VcrT, VciT,
                Dscl, lstep, arv, kpad, SSC, V12, LAM, BPS, VCS, KLP, ATRI,
                SM64, XT2, Z256, D0C, DIB, DRB, MGB, IRB, IIB, PS, PT, BAND,
                trow_dram, TRB)
    emit_l1_main(pb, nc, CS, xn1Td, x1Td, WupT, gwT, ebias, lATf, lBer, outT,
                 XN1, WUP, BAND, LBS, LAS, GWS, EBS, ZGT, X1SL, OUTS, T1C,
                 GSC, ZGW, D0C, PS, PT)
    pb.build()
    pb.ctx.close()
    return nc


# version 09 (final)
# speedup vs baseline: 1.0346x; 1.0346x over previous
import numpy as np
import ml_dtypes
from contextlib import ExitStack

import concourse.bass as bass
import concourse.mybir as mybir
from concourse.bass_utils import run_bass_kernel_spmd

F32 = mybir.dt.float32
BF16 = mybir.dt.bfloat16
AF = mybir.ActivationFunctionType
ALU = mybir.AluOpType
AX = mybir.AxisListType
BFNP = ml_dtypes.bfloat16

NC = 8
D = 4096
SL = 32
TOK = 1024
NT = 64
H = 32
HD = 128
G = 8
KVD = 2048
NEXP, TOPK, RANK = 64, 16, 16
SCALING = 2.0
L1E, L1R = 4, 8
LBIG = -1e30

def own_tiles(c):
    return [c, 15 - c, 16 + c, 31 - c, 32 + c, 47 - c, 48 + c, 63 - c]

def tile_owner_slot(t):
    o = t // 8
    if o == 0: c = t
    elif o == 1: c = 15 - t
    elif o == 2: c = t - 16
    elif o == 3: c = 31 - t
    elif o == 4: c = t - 32
    elif o == 5: c = 47 - t
    elif o == 6: c = t - 48
    else: c = 63 - t
    return c, o

def _consts():
    c = {}
    c["eye"] = np.eye(128, dtype=BFNP)
    c["ones_col"] = np.ones((128, 1), dtype=BFNP)
    c["ones_row_f32"] = np.ones((1, 128), dtype=np.float32)
    b2 = np.zeros((128, 2), dtype=np.float32); b2[:, 0] = 0
    basis2 = np.zeros((128, 2), dtype=np.float32)
    basis2[:, 0] = 1.0
    c["basis2a"] = basis2.astype(BFNP)
    basis2b = np.zeros((128, 2), dtype=np.float32); basis2b[:, 1] = 1.0
    c["basis2b"] = basis2b.astype(BFNP)
    bas32 = np.zeros((128, 32 * 32), dtype=np.float32)
    for h in range(32):
        bas32[:, h * 32 + h] = 1.0
    c["basis32"] = bas32.astype(BFNP)
    L = 4096
    t64 = np.tan(np.pi * np.arange(L, dtype=np.float64) / L).astype(np.float32)
    c["t_row"] = t64.reshape(1, L)
    E = np.exp(2j * np.pi * np.outer(np.arange(64), np.arange(64)) / 64)
    TW = np.exp(2j * np.pi * np.outer(np.arange(64), np.arange(64)) / 4096)
    c["E64r"] = E.real.astype(np.float32)
    c["E64i"] = E.imag.astype(np.float32)
    c["E64i_neg"] = (-E.imag).astype(np.float32)
    c["TWr"] = TW.real.astype(np.float32)
    c["TWi"] = TW.imag.astype(np.float32)
    c["EPr"] = (E.real / L).astype(np.float32)
    c["EPi_neg"] = (-E.imag / L).astype(np.float32)
    c["eye64_f32"] = np.eye(64, dtype=np.float32)
    L = 4096
    t64 = np.tan(np.pi * np.arange(L, dtype=np.float64) / L).astype(np.float32)
    c["t_col"] = np.ascontiguousarray(t64.reshape(32, 128).T)
    c["ones_row_bf"] = np.ones((1, 128), dtype=BFNP)
    c["eye128_f32"] = np.eye(128, dtype=np.float32)
    c["ones64_row_f32"] = np.ones((1, 64), dtype=np.float32)
    return c

def _prep_inputs(inputs):
    x = np.asarray(inputs["L0_attn_x"]).reshape(8192, D)
    xT = np.ascontiguousarray(x.T)
    A0 = np.asarray(inputs["L0_ffn_A"])
    B0 = np.asarray(inputs["L0_ffn_B"])
    Wr = np.asarray(inputs["L0_ffn_Wr"])
    lA = np.asarray(inputs["L1_ffn_lora_A"])
    lB = np.asarray(inputs["L1_ffn_lora_B"])
    gw = np.asarray(inputs["L1_ffn_gate_w"])
    shared = {
        "Wq": np.ascontiguousarray(inputs["L0_attn_Wq"]),
        "Wkv": np.ascontiguousarray(inputs["L0_attn_Wkv"]),
        "Wo": np.ascontiguousarray(inputs["L0_attn_Wo"]),
        "bo": np.ascontiguousarray(np.asarray(inputs["L0_attn_bo"]).reshape(32, 128).T),
        "WrT": np.ascontiguousarray(Wr.T),
        "ATf": np.ascontiguousarray(A0.reshape(1024, D).T),
        "B0kr": np.ascontiguousarray(B0.transpose(0, 2, 1).reshape(1024, D)),
        "WupT": np.ascontiguousarray(np.asarray(inputs["L1_ffn_W_up"]).T),
        "gwT": np.ascontiguousarray(gw.T),
        "ebias": np.asarray(inputs["L1_ffn_expert_bias"]).reshape(1, 4),
        "lATf": np.ascontiguousarray(lA.reshape(32, D).T),
        "lBer": np.ascontiguousarray(lB.transpose(0, 2, 1).reshape(32, D)),
        "Lri": np.asarray(inputs["L1_attn_Lambda_ri"]),
        "pri": np.asarray(inputs["L1_attn_p_ri"]),
        "qri": np.asarray(inputs["L1_attn_q_ri"]),
        "Ctri": np.asarray(inputs["L1_attn_Ct_ri"]).reshape(64, 2),
        "Bp": np.asarray(inputs["L1_attn_Bp"]).reshape(64, 1),
        "VcrT": np.ascontiguousarray(np.asarray(inputs["L1_attn_Vc_ri"])[:, :, 0].T),
        "VciT": np.ascontiguousarray(np.asarray(inputs["L1_attn_Vc_ri"])[:, :, 1].T),
        "Dscl": np.asarray(inputs["L1_attn_D"]).reshape(1, 1),
        "lstep": np.asarray(inputs["L1_attn_log_step"]).reshape(1, 1),
    }
    tri = np.tril(np.ones((128, 128), dtype=np.float32)).T
    per_core = []
    for c in range(NC):
        own = own_tiles(c)
        cols = np.concatenate([np.arange(t * 128, (t + 1) * 128) for t in own])
        m = {"xT": np.ascontiguousarray(xT[:, cols])}
        msk = np.empty((64, 128, 128), dtype=BFNP)
        for kt in range(32):
            T = own[kt // 8]
            msk[kt] = (np.ones((128, 128), np.float32) if kt < T
                       else tri if kt == T else np.zeros((128, 128), np.float32)).astype(BFNP)
        for kt in range(32, 64):
            T = own[4 + (kt - 32) // 8]
            msk[kt] = (np.ones((128, 128), np.float32) if kt < T
                       else tri if kt == T else np.zeros((128, 128), np.float32)).astype(BFNP)
        m["masks"] = msk
        per_core.append(m)
    return shared, per_core

class PB:
    STREAMS = ("t", "v", "s", "g", "y")

    def __init__(self, nc):
        self.nc = nc
        self.streams = {k: [] for k in self.STREAMS}
        self.ctx = ExitStack()
        self.sems = {}
        self.cnt = {}

    def c(self, name):
        return self.cnt.get(name, 0)

    def sem(self, name):
        if name not in self.sems:
            self.sems[name] = self.ctx.enter_context(self.nc.semaphore(name))
            self.cnt[name] = 0
        return self.sems[name]

    def _bump(self, name, n):
        self.sem(name)
        self.cnt[name] += n
        return self.cnt[name]

    def op(self, stream, method, *args, inc=None, incn=1, **kw):
        after = None
        handle = None
        if inc is not None:
            handle = self.sem(inc)
            after = self._bump(inc, incn)
        def f(eng, method=method, args=args, kw=kw, handle=handle, incn=incn):
            ins = getattr(eng, method)(*args, **kw)
            if handle is not None:
                ins.then_inc(handle, incn)
        self.streams[stream].append(f)
        return after

    def wait(self, stream, semname, val):
        handle = self.sem(semname)
        def f(eng, handle=handle, val=val):
            eng.wait_ge(handle, val)
        self.streams[stream].append(f)

    def raw(self, stream, fn):
        self.streams[stream].append(fn)

    def v(self, method, *args, **kw):
        return self.op("v", method, *args, inc="v", **kw)

    def s(self, method, *args, **kw):
        return self.op("s", method, *args, inc="s", **kw)

    def mm(self, out, lhsT, rhs, start, stop):
        return self.op("t", "matmul", out, lhsT, rhs, start=start, stop=stop)

    def tmm(self, out, lhsT, rhs, start, stop):
        return self.op("t", "matmul", out, lhsT, rhs, start=start, stop=stop, inc="t")

    def dma(self, q, sem, out, in_):
        return self.op(q, "dma_start", inc=sem, incn=16, out=out, in_=in_)

    def build(self, name="prog"):
        for sn in sorted(self.cnt):
            if sn.startswith("q") and self.cnt[sn] > 0:
                self.wait("y", sn, self.cnt[sn])
        with self.nc.Block() as block:
            for st in self.STREAMS:
                ops = self.streams[st]
                if not ops:
                    continue
                def make(ops):
                    def body(eng):
                        for f in ops:
                            f(eng)
                    return body
                getattr(block, {"t": "tensor", "v": "vector", "s": "scalar",
                                "g": "gpsimd", "y": "sync"}[st])(make(ops))

def _load_consts(pb, nc, C, names):
    out = {}
    for n in names:
        arr = C[n]
        dt = BF16 if arr.dtype == BFNP else F32
        t = pb.ctx.enter_context(nc.sbuf_tensor("c_" + n, list(arr.shape), dt))
        h = nc.inline_tensor(np.ascontiguousarray(arr), name="i_" + n).ap()
        pb.dma("y", "qc", t[:, :] if len(arr.shape) == 2 else t[:, :, :], h)
        out[n] = t
    return out

def _wpanel_ap(W, j, rows=D, width=128):
    return W[:, j * width:(j + 1) * width].rearrange("(a p) w -> p a w", p=128)

def build_launchA(fused=False):
    nc = bass.Bass("TRN2", target_bir_lowering=False, debug=False)
    C = _consts()
    io = {}
    xT = nc.dram_tensor("xT", [D, TOK], F32, kind="ExternalInput").ap()
    Wq = nc.dram_tensor("Wq", [D, D], F32, kind="ExternalInput").ap()
    Wkv = nc.dram_tensor("Wkv", [D, KVD], F32, kind="ExternalInput").ap()
    WrT = nc.dram_tensor("WrT", [D, 64], F32, kind="ExternalInput").ap()
    ATf = nc.dram_tensor("ATf", [D, 1024], F32, kind="ExternalInput").ap()
    B0kr = nc.dram_tensor("B0kr", [1024, D], F32, kind="ExternalInput").ap()
    kvx = nc.dram_tensor("kvx", [KVD, TOK], BF16, kind="ExternalOutput").ap()
    qT = nc.dram_tensor("qT", [D, TOK], BF16, kind="ExternalOutput").ap()
    moeT = nc.dram_tensor("moeT", [D, TOK], F32, kind="ExternalOutput").ap()

    pb = PB(nc)
    CS = _load_consts(pb, nc, C, ["eye", "ones_col", "ones_row_f32"])

    sb = pb.ctx.enter_context
    XN = sb(nc.sbuf_tensor("XN", [128, SL * 1024], BF16))
    AT = sb(nc.sbuf_tensor("AT", [128, SL * 1024], BF16))
    WPAN = sb(nc.sbuf_tensor("WPAN", [128, 2 * 4096], BF16))
    MBC = sb(nc.sbuf_tensor("MBC", [128, 1024], F32))
    RBC = sb(nc.sbuf_tensor("RBC", [128, 1024], F32))
    HGT = sb(nc.sbuf_tensor("HGT", [128, 8 * 1024], BF16))
    WRTS = sb(nc.sbuf_tensor("WRTS", [128, SL * 64], BF16))
    ARENA = sb(nc.sbuf_tensor("ARENA", [128, 16384], BF16))
    AFV = ARENA[:, :].bitcast(F32)
    XSL = AFV[:, 0:2048]
    TMP = AFV[:, 2048:3072]
    STS = AFV[0:1, 3072:5120]
    ROWS = AFV[0:1, 5120:8192]
    KOUT = ARENA[:, 0:1024]
    VTT = ARENA[:, 1024:2048]
    VOUT = ARENA[:, 2048:3072]
    QOUT = ARENA[:, 3072:5120]
    SC = AFV[:, 2560:3072]
    GTMP = AFV[:, 3072:3584]
    GTMP2 = AFV[:, 3584:4096]
    MOUT = AFV[:, 4096:5120]
    GT = ARENA[:, 10240:10752]
    HGtmp = ARENA[:, 10752:11776]
    B0P = ARENA[:, 11776:13824]
    PS = sb(nc.psum_tensor("PS", [128, 3584], F32))
    PT = sb(nc.psum_tensor("PT", [128, 1024], BF16))

    emit_ln0(pb, nc, CS, xT, XN, XSL, MBC, RBC, TMP, STS, ROWS, PS, 16 * 3)
    for f in range(SL):
        pb.dma("g", "qa", AT[:, f * 1024:(f + 1) * 1024].rearrange("p (a w) -> p a w", a=1),
               ATf[f * 128:(f + 1) * 128, :].unsqueeze(1))
    at_done = pb.cnt["qa"]
    pb.dma("g", "qr", WRTS[:, :].rearrange("p (a w) -> p a w", w=64),
           WrT.rearrange("(a p) w -> p a w", p=128))
    wrt_done = pb.cnt["qr"]

    emit_kv(pb, nc, CS, Wkv, XN, WPAN, KOUT, VTT, VOUT, kvx, PS, PT)
    emit_q(pb, nc, Wq, XN, WPAN, QOUT, qT, PS)
    emit_moe(pb, nc, CS, XN, AT, at_done, WRTS, wrt_done, B0kr, B0P,
             SC, GT, GTMP, GTMP2, HGtmp, HGT, MOUT, moeT, PS, PT)

    pb.build()
    pb.ctx.close()
    return nc

def emit_ln0(pb, nc, CS, xT, XN, XSL, MBC, RBC, TMP, STS, ROWS, PS, consts_done):
    pb.norm_slab = [None] * SL
    """LN0: stream xT slabs, cast to bf16 (XN), stats via matmuls, normalize."""
    sq_cnt = [None, None]
    cast_cnt = [None] * SL
    mm_slab = [None] * SL
    sqv = TMP.bitcast(BF16)
    for i in range(SL):
        par = i % 2
        if i >= 2:
            pb.wait("y", "v", cast_cnt[i - 2])
        ld = pb.dma("y", "qx%d" % par, XSL[:, par * 1024:(par + 1) * 1024],
                    xT[i * 128:(i + 1) * 128, :])
        pb.wait("v", "qx%d" % par, ld)
        cast_cnt[i] = pb.v("tensor_copy", XN[:, i * 1024:(i + 1) * 1024],
                           XSL[:, par * 1024:(par + 1) * 1024])
        pb.wait("s", "v", cast_cnt[i])
        if i >= 2:
            pb.wait("s", "t", mm_slab[i - 2])
        sq_cnt[par] = pb.s("activation", sqv[:, par * 1024:(par + 1) * 1024],
                           XN[:, i * 1024:(i + 1) * 1024], AF.Square)
        pb.wait("t", "s", sq_cnt[par])
        if i == 0:
            pb.wait("t", "qc", consts_done)
        for hf in range(2):
            pb.mm(PS[0:1, hf * 512: (hf + 1) * 512],
                  CS["ones_col"][:, :], XN[:, i * 1024 + hf * 512: i * 1024 + (hf + 1) * 512],
                  start=(i == 0), stop=(i == SL - 1))
            mm_slab[i] = pb.tmm(PS[0:1, 1024 + hf * 512: 1024 + (hf + 1) * 512],
                                CS["ones_col"][:, :], sqv[:, par * 1024 + hf * 512: par * 1024 + (hf + 1) * 512],
                                start=(i == 0), stop=(i == SL - 1))
    stats_done = pb.cnt["t"]
    pb.wait("v", "t", stats_done)
    ev = pb.v("tensor_copy", STS[:, :], PS[0:1, 0:2048])
    m_row = ROWS[0:1, 0:1024]
    r_row = ROWS[0:1, 1024:2048]
    v_row = ROWS[0:1, 2048:3072]
    pb.wait("v", "v", ev)
    e1 = pb.v("tensor_scalar_mul", m_row, STS[0:1, 0:1024], 1.0 / D)
    e2 = pb.v("tensor_scalar_mul", v_row, STS[0:1, 1024:2048], 1.0 / D)
    pb.wait("v", "v", e1)
    c1 = pb.v("tensor_mul", r_row, m_row, m_row)
    pb.wait("v", "v", c1)
    c2 = pb.v("tensor_sub", v_row, v_row, r_row)
    pb.wait("v", "v", c2)
    c2b = pb.v("tensor_scalar_add", STS[0:1, 0:1024], v_row, 1e-5)
    pb.wait("s", "v", c2b)
    c3 = pb.s("activation", v_row, STS[0:1, 0:1024], AF.Sqrt)
    pb.wait("v", "s", c3)
    c4 = pb.v("reciprocal", r_row, v_row)
    pb.wait("t", "v", c4)
    for hf in range(2):
        pb.mm(PS[:, 2048 + hf * 512: 2048 + (hf + 1) * 512], CS["ones_row_f32"][:, :],
              m_row[:, hf * 512:(hf + 1) * 512], start=True, stop=True)
    bc1 = pb.tmm(PS[:, 3072:3584], CS["ones_row_f32"][:, :], r_row[:, 0:512],
                 start=True, stop=True)
    pb.wait("v", "t", bc1)
    pb.v("tensor_copy", MBC[:, :], PS[:, 2048:3072])
    ev2 = pb.v("tensor_copy", RBC[:, 0:512], PS[:, 3072:3584])
    pb.wait("t", "v", ev2)
    bc2 = pb.tmm(PS[:, 3072:3584], CS["ones_row_f32"][:, :], r_row[:, 512:1024],
                 start=True, stop=True)
    pb.wait("v", "t", bc2)
    pb.v("tensor_copy", RBC[:, 512:1024], PS[:, 3072:3584])
    prev_mult = None
    for i in range(SL):
        if prev_mult is not None:
            pb.wait("v", "v", prev_mult)
        t1 = pb.v("tensor_sub", TMP[:, :], XN[:, i * 1024:(i + 1) * 1024], MBC[:, :])
        pb.wait("v", "v", t1)
        prev_mult = pb.v("tensor_mul", XN[:, i * 1024:(i + 1) * 1024], TMP[:, :], RBC[:, :])
        pb.norm_slab[i] = prev_mult
    pb.xn_done = pb.cnt["v"]

def emit_kv(pb, nc, CS, Wkv, XN, WPAN, KOUT, VTT, VOUT, kvx, PS, PT, bg_loads=()):
    pan_cnt = [None, None]
    mmdone = [None, None]
    evdone = [None, None]
    stdone = [None, None]
    for j in range(16):
        par = j % 2
        if j >= 2:
            pb.wait("g", "t", mmdone[par])
        pan_cnt[par] = pb.dma("g", "qw%d" % par, _sb3(WPAN, par), _wpanel_ap(Wkv, j))
        for _ in range(2):
            if bg_loads:
                bg = list(bg_loads).pop(0) if False else None
        if bg_loads:
            bg_loads = list(bg_loads)
            bg_loads.pop(0)()
        pb.wait("t", "qw%d" % par, pan_cnt[par])
        if j >= 2:
            pb.wait("t", "v", evdone[par])
        for hf in range(2):
            ps = PS[:, par * 1024 + hf * 512: par * 1024 + (hf + 1) * 512]
            for f in range(SL):
                if j == 0 and hf == 0:
                    pb.wait("t", "v", pb.norm_slab[f])
                last = (hf == 1) and (f == SL - 1)
                fn = pb.tmm if last else pb.mm
                r = fn(ps, WPAN[:, par * 4096 + f * 128:(par * 4096 + (f + 1) * 128)],
                       XN[:, f * 1024 + hf * 512: f * 1024 + (hf + 1) * 512],
                       start=(f == 0), stop=(f == SL - 1))
        mmdone[par] = r
        g = j // 2
        if j % 2 == 0:
            pb.wait("v", "t", mmdone[par])
            if stdone[par] is not None:
                pb.wait("v", "qk", stdone[par])
            evdone[par] = pb.v("tensor_copy", KOUT[:, :], PS[:, par * 1024: par * 1024 + 1024])
            pb.wait("y", "v", evdone[par])
            stdone[par] = pb.dma("y", "qk", kvx[g * 128:(g + 1) * 128, :], KOUT[:, :])
        else:
            pb.wait("v", "t", mmdone[par])
            evdone[par] = pb.v("tensor_copy", VTT[:, :], PS[:, par * 1024: par * 1024 + 1024])
            pb.wait("t", "v", evdone[par])
            trd = None
            for tt in range(8):
                trd = pb.op("t", "transpose", PT[:, tt * 128:(tt + 1) * 128],
                            VTT[:, tt * 128:(tt + 1) * 128], CS["eye"][:, :], inc="t")
            pb.wait("v", "t", trd)
            if stdone[par] is not None:
                pb.wait("v", "qvs", stdone[par])
            vod = pb.v("tensor_copy", VOUT[:, :], PT[:, :])
            evdone[par] = vod
            pb.wait("y", "v", vod)
            dst = kvx[1024:2048, g * 128:(g + 1) * 128].rearrange("(a p) w -> p a w", p=128)
            stdone[par] = pb.dma("y", "qvs", dst, VOUT[:, :].rearrange("p (a w) -> p a w", w=128))

def _sb3(buf, par, width=4096):
    return buf[:, par * width:(par + 1) * width].rearrange("p (a w) -> p a w", a=32)

def emit_q(pb, nc, Wq, XN, WPAN, QOUT, qT, PS, bg_loads=()):
    pan_cnt = [None, None]
    mmdone = [None, None]
    evdone = [None, None]
    stq = [None, None]
    for j in range(SL):
        par = j % 2
        pb.wait("g", "t", mmdone[par] if mmdone[par] else pb.cnt["t"])
        pan_cnt[par] = pb.dma("g", "qw%d" % par, _sb3(WPAN, par), _wpanel_ap(Wq, j))
        if bg_loads:
            bg_loads = list(bg_loads)
            bg_loads.pop(0)()
        pb.wait("t", "qw%d" % par, pan_cnt[par])
        if evdone[par] is not None:
            pb.wait("t", "v", evdone[par])
        for hf in range(2):
            ps = PS[:, par * 1024 + hf * 512: par * 1024 + (hf + 1) * 512]
            for f in range(SL):
                last = (hf == 1) and (f == SL - 1)
                fn = pb.tmm if last else pb.mm
                r = fn(ps, WPAN[:, par * 4096 + f * 128: par * 4096 + (f + 1) * 128],
                       XN[:, f * 1024 + hf * 512: f * 1024 + (hf + 1) * 512],
                       start=(f == 0), stop=(f == SL - 1))
        mmdone[par] = r
        pb.wait("v", "t", mmdone[par])
        if stq[par] is not None:
            pb.wait("v", "qq%d" % par, stq[par])
        evdone[par] = pb.v("tensor_copy", QOUT[:, par * 1024:(par + 1) * 1024],
                           PS[:, par * 1024: par * 1024 + 1024])
        pb.wait("y", "v", evdone[par])
        stq[par] = pb.dma("y", "qq%d" % par, qT[j * 128:(j + 1) * 128, :],
                          QOUT[:, par * 1024:(par + 1) * 1024])

def emit_moe(pb, nc, CS, XN, AT, at_done, WRTS, wrt_done, B0kr, B0P,
             SC, GT, GTMP, GTMP2, HGtmp, HGT, MOUT, moeT, PS, PT):
    pb.wait("t", "v", pb.c("v"))
    gate_done = [None] * 8
    prev_ev = None
    prev_a2 = None
    for tt in range(8):
        if tt == 0:
            pb.wait("t", "qr", wrt_done)
        ps = PS[:, 0:64]
        if prev_ev is not None:
            pb.wait("t", "v", prev_ev)
        for f in range(SL):
            fn = pb.tmm if f == SL - 1 else pb.mm
            smm = fn(ps, XN[:, f * 1024 + tt * 128: f * 1024 + (tt + 1) * 128],
                     WRTS[:, f * 64:(f + 1) * 64], start=(f == 0), stop=(f == SL - 1))
        pb.wait("v", "t", smm)
        if prev_a2 is not None:
            pb.wait("v", "s", prev_a2)
        sc = SC[:, tt * 64:(tt + 1) * 64]
        prev_ev = e1 = pb.v("tensor_copy", sc, ps)
        pb.wait("v", "v", e1)
        mx = GTMP[:, 64:72]
        e2 = pb.v("max", out=mx, in_=sc)
        pb.wait("v", "v", e2)
        e3 = pb.v("match_replace", out=GTMP[:, 128:192], in_to_replace=mx,
                  in_values=sc, imm_value=LBIG)
        pb.wait("v", "v", e3)
        e4 = pb.v("max", out=GTMP[:, 72:80], in_=GTMP[:, 128:192])
        pb.wait("v", "v", e4)
        e5 = pb.v("match_replace", out=GTMP[:, 192:256], in_to_replace=GTMP[:, 72:80],
                  in_values=GTMP[:, 128:192], imm_value=LBIG)
        negm = GTMP[:, 80:81]
        nm = pb.v("tensor_scalar_mul", negm, mx[:, 0:1], -1.0)
        pb.wait("s", "v", max(e5, nm))
        a1 = pb.s("activation", GTMP2[:, 0:64], sc, AF.Exp, bias=negm)
        prev_a2 = a2 = pb.s("activation", GTMP2[:, 64:128], GTMP[:, 192:256],
                            AF.Exp, bias=negm)
        pb.wait("v", "s", a2)
        zcol = GTMP[:, 82:83]
        e6a = pb.v("tensor_sub", GTMP2[:, 128:192], GTMP2[:, 0:64], GTMP2[:, 64:128])
        pb.wait("v", "v", e6a)
        e6 = pb.v("tensor_reduce", zcol, GTMP2[:, 128:192], axis=AX.X, op=ALU.add)
        pb.wait("v", "v", e6)
        e7 = pb.v("reciprocal", GTMP[:, 84:85], zcol)
        pb.wait("v", "v", e7)
        gate_done[tt] = pb.v("tensor_scalar", GT[:, tt * 64:(tt + 1) * 64],
                             GTMP2[:, 128:192], GTMP[:, 84:85], SCALING,
                             op0=ALU.mult, op1=ALU.mult)
    pb.wait("t", "qa", at_done)
    hg_ev = [None, None]
    for tt in range(8):
        for hf in range(2):
            par = hf
            ps = PS[:, 1024 + par * 512: 1024 + (par + 1) * 512]
            if hg_ev[par] is not None:
                pb.wait("t", "v", hg_ev[par])
            for f in range(SL):
                fn = pb.tmm if f == SL - 1 else pb.mm
                hmm = fn(ps, XN[:, f * 1024 + tt * 128: f * 1024 + (tt + 1) * 128],
                         AT[:, f * 1024 + hf * 512: f * 1024 + (hf + 1) * 512],
                         start=(f == 0), stop=(f == SL - 1))
            pb.wait("v", "t", hmm)
            pb.wait("v", "v", gate_done[tt])
            gsl = GT[:, tt * 64 + hf * 32: tt * 64 + (hf + 1) * 32]
            gbc = gsl.unsqueeze(2).broadcast_to([128, 32, 16])
            hg_ev[par] = pb.v("tensor_tensor",
                              HGtmp[:, hf * 512:(hf + 1) * 512].rearrange("p (k r) -> p k r", k=32),
                              ps.rearrange("p (k r) -> p k r", k=32), gbc, op=ALU.mult)
        pb.wait("t", "v", hg_ev[1])
        if tt > 0:
            pb.wait("t", "v", pt_ev)
        trp = None
        for krc in range(8):
            trp = pb.op("t", "transpose", PT[:, krc * 128:(krc + 1) * 128],
                        HGtmp[:, krc * 128:(krc + 1) * 128], CS["eye"][:, :], inc="t")
        pb.wait("v", "t", trp)
        pt_ev = pb.v("tensor_copy",
                     HGT[:, :].rearrange("p (a w) -> p a w", w=1024)[:, :, tt * 128:(tt + 1) * 128],
                     PT[:, :].rearrange("p (a w) -> p a w", w=128))
    hgt_done = pb.cnt["v"]
    b0_cnt = [None, None]
    momm = [None, None]
    moev = [None, None]
    most = [None, None]
    for od in range(SL):
        par = od % 2
        if momm[par] is not None:
            pb.wait("g", "t", momm[par])
        b0_cnt[par] = pb.dma("g", "qb%d" % par, B0P[:, par * 1024:(par + 1) * 1024]
                             .rearrange("p (a w) -> p a w", w=128),
                             B0kr[:, od * 128:(od + 1) * 128]
                             .rearrange("(a p) w -> p a w", p=128))
        pb.wait("t", "qb%d" % par, b0_cnt[par])
        if od == 0:
            pb.wait("t", "v", hgt_done)
        if moev[par] is not None:
            pb.wait("t", "v", moev[par])
        for hf in range(2):
            ps = PS[:, 1536 + par * 1024 + hf * 512: 1536 + par * 1024 + (hf + 1) * 512]
            for krc in range(8):
                last = (hf == 1) and (krc == 7)
                fn = pb.tmm if last else pb.mm
                r = fn(ps, B0P[:, par * 1024 + krc * 128: par * 1024 + (krc + 1) * 128],
                       HGT[:, krc * 1024 + hf * 512: krc * 1024 + (hf + 1) * 512],
                       start=(krc == 0), stop=(krc == 7))
        momm[par] = r
        pb.wait("v", "t", momm[par])
        if most[0] is not None:
            pb.wait("v", "qm", most[0])
        moev[par] = pb.v("tensor_copy", MOUT[:, :],
                         PS[:, 1536 + par * 1024: 1536 + par * 1024 + 1024])
        pb.wait("y", "v", moev[par])
        most[0] = pb.dma("y", "qm", moeT[od * 128:(od + 1) * 128, :], MOUT[:, :])

_prog_cache = {}

def _get_prog(name, builder):
    if name not in _prog_cache:
        _prog_cache[name] = builder()
    return _prog_cache[name]

def kernel(**inputs):
    shared, per_core = _prep_inputs(inputs)
    cores = list(range(NC))
    nc = _get_prog("F", build_fused)
    in_maps = []
    for c in range(NC):
        m = {"xT": per_core[c]["xT"], "masks": per_core[c]["masks"]}
        for n in ("Wq", "Wkv", "WrT", "ATf", "B0kr", "Wo", "bo", "WupT", "gwT",
                  "ebias", "lATf", "lBer", "Lri", "pri", "qri", "Ctri", "Bp",
                  "VcrT", "VciT", "Dscl", "lstep"):
            m[n] = shared[n]
        in_maps.append(m)
    res = run_bass_kernel_spmd(nc, in_maps, core_ids=cores)
    out = np.empty((8192, D), dtype=np.float32)
    for c in range(NC):
        o = np.asarray(res.results[c]["outT"])
        for s, t in enumerate(own_tiles(c)):
            out[t * 128:(t + 1) * 128, :] = o[:, s * 128:(s + 1) * 128].T
    return out.reshape(1, 8192, D)

SCALE_ATT = float(1.0 / np.sqrt(128.0))

def _scol(p, kt):
    if p == 0:
        return 128 * (kt // 8)
    return 0 if kt < 32 else 128 * ((kt - 32) // 8)

def build_launchB(fused=False):
    nc = bass.Bass("TRN2", target_bir_lowering=False, debug=False)
    C = _consts()
    qT = nc.dram_tensor("qT", [D, TOK], BF16, kind="ExternalInput").ap()
    kvg = nc.dram_tensor("kvg", [8 * KVD, TOK], BF16, kind="ExternalInput").ap()
    xT = nc.dram_tensor("xT", [D, TOK], F32, kind="ExternalInput").ap()
    moeT = nc.dram_tensor("moeT", [D, TOK], F32, kind="ExternalInput").ap()
    Wo = nc.dram_tensor("Wo", [D, D], F32, kind="ExternalInput").ap()
    bo = nc.dram_tensor("bo", [128, 32], F32, kind="ExternalInput").ap()
    masks = nc.dram_tensor("masks", [64, 128, 128], BF16, kind="ExternalInput").ap()
    x1T = nc.dram_tensor("x1T", [D, TOK], F32, kind="ExternalOutput").ap()
    xn1T = nc.dram_tensor("xn1T", [D, TOK], BF16, kind="ExternalOutput").ap()
    rbounce = nc.dram_tensor("rbounce", [32, 512], F32, kind="Internal").ap()

    pb = PB(nc)
    CS = _load_consts(pb, nc, C, ["eye", "ones_col", "ones_row_f32", "basis32"])
    consts_done = 16 * 4

    sb = pb.ctx.enter_context
    QP = sb(nc.sbuf_tensor("QP", [128, 32 * 512], BF16))
    CTX = sb(nc.sbuf_tensor("CTX", [128, 32 * 512], F32))
    KB = sb(nc.sbuf_tensor("KB", [128, 2 * 4 * 1024], BF16))
    VB = sb(nc.sbuf_tensor("VB", [128, 2 * 4 * 1024], BF16))
    PSB = sb(nc.sbuf_tensor("PSB", [128, 2 * 1024], BF16))
    MKB = sb(nc.sbuf_tensor("MKB", [128, 2 * 4 * 128], BF16))
    WOP = sb(nc.sbuf_tensor("WOP", [128, 2 * 4096], BF16))
    RCP = sb(nc.sbuf_tensor("RCP", [32, 512], F32))
    RCROW = sb(nc.sbuf_tensor("RCROW", [1, 2 * 512], F32))
    RB = sb(nc.sbuf_tensor("RB", [128, 512], F32))
    BOC = sb(nc.sbuf_tensor("BOC", [128, 32], F32))
    XSL2 = sb(nc.sbuf_tensor("XSL2", [128, 2 * 512], F32))
    MSL = sb(nc.sbuf_tensor("MSL", [128, 2 * 512], F32))
    T1 = sb(nc.sbuf_tensor("T1", [128, 512], F32))
    X1O = sb(nc.sbuf_tensor("X1O", [128, 2 * 512], F32))
    X1B = sb(nc.sbuf_tensor("X1B", [128, 512], BF16))
    SQB = sb(nc.sbuf_tensor("SQB", [128, 512], BF16))
    STS2 = sb(nc.sbuf_tensor("STS2", [1, 2048], F32))
    ROW2 = sb(nc.sbuf_tensor("ROW2", [1, 3072], F32))
    MB2 = sb(nc.sbuf_tensor("MB2", [128, 1024], F32))
    RB2 = sb(nc.sbuf_tensor("RB2", [128, 1024], F32))
    RCROW8 = KB[:, :].bitcast(F32)[0:1, 0:16384]
    XR = KB[:, :].bitcast(F32)[:, 0:2048]
    XNO = VB[:, 0:2048]
    PS = sb(nc.psum_tensor("PS", [128, 4096], F32))

    pb.dma("y", "qc", BOC[:, :], bo)
    consts_done += 16

    x1st = emit_attention(pb, nc, CS, consts_done, qT, kvg, xT, moeT, Wo, masks,
                          QP, CTX, KB, VB, PSB, MKB, WOP, RCP, RCROW, rbounce,
                          RB, BOC, XSL2, MSL, T1, X1O, X1B, SQB, STS2, PS, x1T,
                          PS[0:1, 3584:4096], RCROW8=RCROW8)
    emit_xn1(pb, nc, CS, x1T, xn1T, STS2, ROW2, MB2, RB2, XR, XNO, PS, x1st)

    pb.build()
    pb.ctx.close()
    return nc

def emit_attention(pb, nc, CS, consts_done, qT, kvg, xT, moeT, Wo, masks,
                   QP, CTX, KB, VB, PSB, MKB, WOP, RCP, RCROW, rbounce,
                   RB, BOC, XSL2, MSL, T1, X1O, X1B, SQB, STS2, PS, x1T, STP,
                   RCROW8=None):
    BAS = CS["basis32"]
    x1_store_cnt = []
    for p in range(2):
        nkt = 32 if p == 0 else 64
        qcols = slice(p * 512, (p + 1) * 512)
        if p == 1:
            pb.wait("y", "t", pb.c("t"))
            pb.wait("t", "v", pb.c("v"))
        qld = pb.dma("y", "qqp", QP[:, :].rearrange("p (a w) -> p a w", w=512),
                     qT[:, qcols].rearrange("(a p) w -> p a w", p=128))
        nch = nkt // 4
        kv_cnt = [None, None]
        mk_cnt = [None, None]
        kv_cnt[0], mk_cnt[0] = _kv_chunk_load(pb, kvg, masks, p, 0, KB, VB, MKB)
        ctx_ev = {}
        first_sums = True
        for ch in range(nch):
            cpar = ch % 2
            if ch + 1 < nch:
                pb.wait("y", "t", pb.c("t"))
                kv_cnt[1 - cpar], mk_cnt[1 - cpar] = _kv_chunk_load(
                    pb, kvg, masks, p, ch + 1, KB, VB, MKB)
            items = [(hg, ki) for hg in range(16) for ki in range(4)]
            first_kt = 4 * ch
            scol0 = _scol(p, first_kt)

            def emit_scores(item, idx):
                hg, ki = item
                kt = 4 * ch + ki
                par = idx % 2
                sc = _scol(p, kt)
                sbank = 1024 + par * 1024
                key = par
                if idx - 2 in exp_by_idx:
                    pb.wait("t", "s", exp_by_idx[idx - 2])
                if idx == 0:
                    pb.wait("t", "qkb%d" % cpar, kv_cnt[cpar][0])
                    if mk_cnt[cpar] is not None:
                        pb.wait("t", "qmk%d" % cpar, mk_cnt[cpar])
                    if ch == 0:
                        pb.wait("t", "qqp", qld)
                        if p == 0:
                            pb.wait("t", "qc", consts_done)
                sm = None
                for hp, h in ((0, 2 * hg), (1, 2 * hg + 1)):
                    g = h // 4
                    lhsT = KB[:, cpar * 4096 + ki * 1024 + g * 128:
                              cpar * 4096 + ki * 1024 + (g + 1) * 128]
                    rhs = QP[:, h * 512 + sc:(h + 1) * 512]
                    sm = pb.tmm(PS[:, sbank + hp * 512 + sc: sbank + (hp + 1) * 512],
                                lhsT, rhs, start=True, stop=True)
                return sm

            exp_by_idx = {}
            score_by_idx = {}
            sums_by_idx = {}
            mask_by_idx = {}

            def emit_ctx_sums(jdx):
                nonlocal first_sums
                hgj, kij = items[jdx]
                ktj = 4 * ch + kij
                parj = jdx % 2
                scj = _scol(p, ktj)
                if mask_by_idx.get(jdx) is not None:
                    pb.wait("t", "v", mask_by_idx[jdx])
                else:
                    pb.wait("t", "s", exp_by_idx[jdx])
                if kij == 0 and ctx_ev.get("last") is not None:
                    pb.wait("t", "v", ctx_ev["last"])
                ctx_done = None
                for hp, h in ((0, 2 * hgj), (1, 2 * hgj + 1)):
                    g = h // 4
                    vt = VB[:, cpar * 4096 + kij * 1024 + g * 128:
                            cpar * 4096 + kij * 1024 + (g + 1) * 128]
                    prhs = PSB[:, parj * 1024 + hp * 512 + scj: parj * 1024 + (hp + 1) * 512]
                    fn = pb.tmm if (kij == 3 and hp == 1) else pb.mm
                    ctx_done = fn(PS[:, hp * 512 + scj:(hp + 1) * 512], vt, prhs,
                                  start=(kij == 0), stop=(kij == 3))
                for hp, h in ((0, 2 * hgj), (1, 2 * hgj + 1)):
                    prhs = PSB[:, parj * 1024 + hp * 512 + scj: parj * 1024 + (hp + 1) * 512]
                    last_sums = (ch == nch - 1 and hgj == 15 and kij == 3 and hp == 1)
                    sl = pb.tmm(PS[0:32, 3072 + scj:3584], BAS[:, h * 32:(h + 1) * 32],
                                prhs, start=first_sums, stop=last_sums)
                    first_sums = False
                sums_by_idx[jdx] = sl
                if kij == 3:
                    pb.wait("v", "t", ctx_done)
                    n0 = 512 - scol0
                    dst = CTX[:, 2 * hgj * 512:].rearrange(
                        "p (a w) -> p a w", w=512)[:, 0:2, scol0:512]
                    src_ = PS[:, 0:1024].rearrange(
                        "p (a w) -> p a w", w=512)[:, :, scol0:512]
                    if ch == 0:
                        ctx_ev["last"] = pb.v("tensor_copy", dst, src_)
                    else:
                        ctx_ev["last"] = pb.v("tensor_tensor", dst, dst, src_, op=ALU.add)

            score_by_idx[0] = emit_scores(items[0], 0)
            for idx, item in enumerate(items):
                hg, ki = item
                kt = 4 * ch + ki
                par = idx % 2
                sbank = 1024 + par * 1024
                pb.wait("s", "t", score_by_idx[idx])
                if idx - 2 in sums_by_idx:
                    pb.wait("s", "t", sums_by_idx[idx - 2])
                sc_i = _scol(p, kt)
                e = pb.s("activation",
                         PSB[:, par * 1024:(par + 1) * 1024].rearrange(
                             "p (a w) -> p a w", w=512)[:, :, sc_i:512],
                         PS[:, sbank: sbank + 1024].rearrange(
                             "p (a w) -> p a w", w=512)[:, :, sc_i:512],
                         AF.Exp, scale=SCALE_ATT)
                exp_by_idx[idx] = e
                if idx + 1 < len(items):
                    score_by_idx[idx + 1] = emit_scores(items[idx + 1], idx + 1)
                need_mask = (p == 0) or (kt >= 32)
                if need_mask:
                    o = (kt // 8) if p == 0 else ((kt - 32) // 8)
                    pb.wait("v", "s", e)
                    pv = PSB[:, par * 1024:(par + 1) * 1024].rearrange(
                        "p (a w) -> p a w", w=512)[:, :, o * 128:(o + 1) * 128]
                    mk = MKB[:, cpar * 512 + ki * 128:(cpar * 512 + (ki + 1) * 128)]
                    mask_by_idx[idx] = pb.v("tensor_tensor", pv, pv,
                                            mk.unsqueeze(1).broadcast_to([128, 2, 128]),
                                            op=ALU.mult)
                if idx >= 1:
                    emit_ctx_sums(idx - 1)
            emit_ctx_sums(len(items) - 1)
        pb.wait("v", "t", pb.c("t"))
        if p == 1:
            pb.wait("v", "qrb", pb.c("qrb"))
        rc = pb.v("reciprocal", RCP[:, :], PS[0:32, 3072:3584])
        pb.wait("y", "v", rc)
        b1 = pb.dma("y", "qrb", rbounce[:, :], RCP[:, :])
        pb.wait("y", "qrb", b1)
        nrm_done = _emit_normalize_ctx(pb, nc, CS, rc, CTX, QP, RB, PS, RCROW, rbounce,
                                       RCROW8)
        x1_store_cnt.append(
            _emit_wo_residual(pb, nc, CS, p, Wo, xT, moeT, QP, WOP, BOC,
                              XSL2, MSL, T1, X1O, X1B, SQB, STS2, PS, x1T, STP))
    return x1_store_cnt

def _kv_chunk_load(pb, kvg, masks, p, ch, KB, VB, MKB):
    cpar = ch % 2
    cnts = []
    need_mask = None
    for ki in range(4):
        kt = 4 * ch + ki
        t = kt if p == 0 else kt
        o, s = tile_owner_slot(t)
        base = 2048 * o
        kin = kvg[base: base + 1024, 128 * s: 128 * (s + 1)].rearrange(
            "(a p) w -> p a w", p=128)
        cnts.append(pb.dma("y", "qkb%d" % cpar,
                           KB[:, cpar * 4096 + ki * 1024:]
                           .rearrange("p (a w) -> p a w", w=128)[:, 0:8, :], kin))
        vin = kvg[base + 1024 + 128 * s: base + 1024 + 128 * (s + 1), :].rearrange(
            "p (a w) -> p a w", w=128)
        cnts.append(pb.dma("y", "qkb%d" % cpar,
                           VB[:, cpar * 4096 + ki * 1024:]
                           .rearrange("p (a w) -> p a w", w=128)[:, 0:8, :], vin))
    mk_cnt = None
    if p == 0 or ch >= 8:
        mrow = 4 * ch if p == 0 else 4 * ch
        mk_cnt = pb.dma("y", "qmk%d" % cpar,
                        MKB[:, cpar * 512:].rearrange("p (a w) -> p a w", w=128)[:, 0:4, :],
                        masks[mrow: mrow + 4, :, :].rearrange("a p w -> p a w"))
    return cnts[-1:] + [None], mk_cnt

def _emit_normalize_ctx(pb, nc, CS, rc, CTX, QP, RB, PS, RCROW, rbounce, RCROW8):
    prev_ev = None
    prev_mul = None
    rld = [None, None]
    pb.wait("y", "t", pb.c("t"))
    gld = {}

    def load_grp(grp, wait_t=None):
        sl_ = grp % 2
        if wait_t is not None:
            pb.wait("y", "t", wait_t)
        gld[grp] = pb.dma("y", "qrr%d" % sl_,
                          RCROW8[0:1, sl_ * 4096:(sl_ + 1) * 4096].rearrange(
                              "o (a w) -> o a w", w=512),
                          rbounce[grp * 8:(grp + 1) * 8, :].unsqueeze(0).rearrange("b a w -> b a w"))
    load_grp(0)
    load_grp(1)
    last_bc = None
    for h in range(32):
        par = h % 2
        grp = h // 8
        sl_ = grp % 2
        bank = PS[:, par * 512:(par + 1) * 512]
        if prev_ev is not None:
            pb.wait("t", "v", prev_ev)
        if h % 8 == 0:
            if grp + 2 <= 3 and h > 0:
                pass
            pb.wait("t", "qrr%d" % sl_, gld[grp])
        bc = pb.tmm(bank, CS["ones_row_f32"][:, :],
                    RCROW8[0:1, sl_ * 4096 + (h % 8) * 512: sl_ * 4096 + (h % 8 + 1) * 512],
                    start=True, stop=True)
        if h % 8 == 7 and (grp + 2) <= 3:
            load_grp(grp + 2, wait_t=bc)
        pb.wait("v", "t", bc)
        if prev_mul is not None:
            pb.wait("v", "v", prev_mul)
        prev_ev = pb.v("tensor_copy", RB[:, :], bank)
        pb.wait("v", "v", prev_ev)
        prev_mul = pb.v("tensor_mul", QP[:, h * 512:(h + 1) * 512].bitcast(BF16) if False
                        else QP[:, h * 512:(h + 1) * 512],
                        CTX[:, h * 512:(h + 1) * 512], RB[:, :])
    return prev_mul

def _emit_wo_residual(pb, nc, CS, p, Wo, xT, moeT, QP, WOP, BOC,
                      XSL2, MSL, T1, X1O, X1B, SQB, STS2, PS, x1T, STP):
    pb.wait("t", "v", pb.c("v"))
    qcols = slice(p * 512, (p + 1) * 512)
    wo_cnt = [None, None]
    momm = [None, None]
    xld = [None, None]
    mld = [None, None]
    stx = [None, None]
    evd = [None, None]
    sq_prev = None
    t_entry = pb.c("t")
    for od in range(SL):
        par = od % 2
        pb.wait("g", "t", momm[par] if momm[par] is not None else t_entry)
        wo_cnt[par] = pb.dma("g", "qwo%d" % par, _sb3(WOP, par), _wpanel_ap(Wo, od))
        if xld[par] is not None:
            pb.wait("y", "v", evd[par])
        xld[par] = pb.dma("y", "qxr%d" % par, XSL2[:, par * 512:(par + 1) * 512],
                          xT[od * 128:(od + 1) * 128, qcols])
        mld[par] = pb.dma("y", "qmr%d" % par, MSL[:, par * 512:(par + 1) * 512],
                          moeT[od * 128:(od + 1) * 128, qcols])
        pb.wait("t", "qwo%d" % par, wo_cnt[par])
        if evd[par] is not None:
            pb.wait("t", "v", evd[par])
        ps = PS[:, par * 512:(par + 1) * 512]
        for f in range(SL):
            fn = pb.tmm if f == SL - 1 else pb.mm
            r = fn(ps, WOP[:, par * 4096 + f * 128: par * 4096 + (f + 1) * 128],
                   QP[:, f * 512:(f + 1) * 512], start=(f == 0), stop=(f == SL - 1))
        momm[par] = r
        pb.wait("v", "t", momm[par])
        pb.wait("v", "qxr%d" % par, xld[par])
        pb.wait("v", "qmr%d" % par, mld[par])
        if sq_prev is not None:
            pb.wait("v", "s", sq_prev)
        t1 = pb.v("scalar_tensor_tensor", T1[:, :], ps, BOC[:, od:od + 1],
                  MSL[:, par * 512:(par + 1) * 512], op0=ALU.add, op1=ALU.add)
        pb.wait("v", "v", t1)
        pb.wait("v", "qx1%d" % par,
                stx[par] if stx[par] is not None else pb.c("qx1%d" % par))
        x1 = pb.v("tensor_add", X1O[:, par * 512:(par + 1) * 512], T1[:, :],
                  XSL2[:, par * 512:(par + 1) * 512])
        pb.wait("v", "v", x1)
        evd[par] = xb = pb.v("tensor_copy", X1B[:, :], X1O[:, par * 512:(par + 1) * 512])
        pb.wait("y", "v", x1)
        stx[par] = pb.dma("y", "qx1%d" % par, x1T[od * 128:(od + 1) * 128, qcols],
                          X1O[:, par * 512:(par + 1) * 512])
        pb.wait("s", "v", xb)
        sq_prev = sq = pb.s("activation", SQB[:, :], X1B[:, :], AF.Square)
        pb.wait("t", "s", sq)
        pb.mm(STP, CS["ones_col"][:, :], X1B[:, :],
              start=(od == 0), stop=(od == SL - 1))
        st2 = pb.tmm(PS[0:1, 3072:3584], CS["ones_col"][:, :], SQB[:, :],
                     start=(od == 0), stop=(od == SL - 1))
    pb.wait("v", "t", st2)
    ev1 = pb.v("tensor_copy", STS2[0:1, p * 512:(p + 1) * 512], STP)
    ev2 = pb.v("tensor_copy", STS2[0:1, 1024 + p * 512: 1024 + (p + 1) * 512],
               PS[0:1, 3072:3584])
    pb.stats_ev = ev2
    return stx

def emit_xn1(pb, nc, CS, x1T, xn1T, STS2, ROW2, MB2, RB2, XR, XNO, PS, x1st,
             VROW=None):
    pb.wait("v", "v", pb.stats_ev)
    m_row = ROW2[0:1, 0:1024]
    r_row = ROW2[0:1, 1024:2048]
    v_row = ROW2[0:1, 2048:3072] if VROW is None else VROW
    e1 = pb.v("tensor_scalar_mul", m_row, STS2[0:1, 0:1024], 1.0 / D)
    e2 = pb.v("tensor_scalar_mul", v_row, STS2[0:1, 1024:2048], 1.0 / D)
    pb.wait("v", "v", e1)
    c1 = pb.v("tensor_mul", r_row, m_row, m_row)
    pb.wait("v", "v", max(c1, e2))
    c2 = pb.v("tensor_sub", v_row, v_row, r_row)
    pb.wait("v", "v", c2)
    c2b = pb.v("tensor_scalar_add", STS2[0:1, 0:1024], v_row, 1e-5)
    pb.wait("s", "v", c2b)
    c3 = pb.s("activation", v_row, STS2[0:1, 0:1024], AF.Sqrt)
    pb.wait("v", "s", c3)
    c4 = pb.v("reciprocal", r_row, v_row)
    pb.wait("t", "v", c4)
    pb.wait("t", "v", pb.c("v"))
    for hf in range(2):
        pb.mm(PS[:, hf * 512:(hf + 1) * 512], CS["ones_row_f32"][:, :],
              m_row[:, hf * 512:(hf + 1) * 512], start=True, stop=True)
        bc = pb.tmm(PS[:, 1024 + hf * 512: 1024 + (hf + 1) * 512],
                    CS["ones_row_f32"][:, :], r_row[:, hf * 512:(hf + 1) * 512],
                    start=True, stop=True)
    pb.wait("v", "t", bc)
    pb.v("tensor_copy", MB2[:, :], PS[:, 0:1024])
    bce = pb.v("tensor_copy", RB2[:, :], PS[:, 1024:2048])
    for sn in ("qx10", "qx11"):
        if sn in pb.cnt:
            pb.wait("y", sn, pb.cnt[sn])
    ld = [None, None]
    sub_prev = [None, None]
    pb.xn1_slab_done = [None] * SL
    for i in range(SL):
        par = i % 2
        if sub_prev[par] is not None:
            pb.wait("y", "v", sub_prev[par])
        ld[par] = pb.dma("y", "qs1%d" % par, XR[:, par * 1024:(par + 1) * 1024],
                         x1T[i * 128:(i + 1) * 128, :])
        pb.wait("v", "qs1%d" % par, ld[par])
        if i == 0:
            pb.wait("v", "v", bce)
        t1 = pb.v("tensor_sub", XR[:, par * 1024:(par + 1) * 1024],
                  XR[:, par * 1024:(par + 1) * 1024], MB2[:, :])
        pb.wait("v", "v", t1)
        mu = pb.v("tensor_mul", XNO[:, i * 1024:(i + 1) * 1024],
                  XR[:, par * 1024:(par + 1) * 1024], RB2[:, :])
        sub_prev[par] = mu
        pb.xn1_slab_done[i] = mu

def build_launchC(fused=False):
    nc = bass.Bass("TRN2", target_bir_lowering=False, debug=False)
    C = _consts()
    xn1T = nc.dram_tensor("xn1T", [D, TOK], BF16, kind="ExternalInput").ap()
    x1T = nc.dram_tensor("x1T", [D, TOK], F32, kind="ExternalInput").ap()
    WupT = nc.dram_tensor("WupT", [D, D], F32, kind="ExternalInput").ap()
    gwT = nc.dram_tensor("gwT", [D, 4], F32, kind="ExternalInput").ap()
    ebias = nc.dram_tensor("ebias", [1, 4], F32, kind="ExternalInput").ap()
    lATf = nc.dram_tensor("lATf", [D, 32], F32, kind="ExternalInput").ap()
    lBer = nc.dram_tensor("lBer", [32, D], F32, kind="ExternalInput").ap()
    Lri = nc.dram_tensor("Lri", [64, 2], F32, kind="ExternalInput").ap()
    pri = nc.dram_tensor("pri", [64, 2], F32, kind="ExternalInput").ap()
    qri = nc.dram_tensor("qri", [64, 2], F32, kind="ExternalInput").ap()
    Ctri = nc.dram_tensor("Ctri", [64, 2], F32, kind="ExternalInput").ap()
    Bp = nc.dram_tensor("Bp", [64, 1], F32, kind="ExternalInput").ap()
    VcrT = nc.dram_tensor("VcrT", [64, 64], F32, kind="ExternalInput").ap()
    VciT = nc.dram_tensor("VciT", [64, 64], F32, kind="ExternalInput").ap()
    Dscl = nc.dram_tensor("Dscl", [1, 1], F32, kind="ExternalInput").ap()
    lstep = nc.dram_tensor("lstep", [1, 1], F32, kind="ExternalInput").ap()
    outT = nc.dram_tensor("outT", [D, TOK], F32, kind="ExternalOutput").ap()
    arv = nc.dram_tensor("arv", [2, 4096], F32, kind="Internal").ap()
    kpad = nc.dram_tensor("kpad", [4352], BF16, kind="Internal").ap()

    pb = PB(nc)
    CS = _load_consts(pb, nc, C, ["eye", "ones_col", "ones_row_f32", "t_row",
                                  "t_col", "E64r", "E64i", "E64i_neg", "TWr",
                                  "TWi", "EPr", "EPi_neg", "eye64_f32",
                                  "ones_row_bf", "ones64_row_f32", "eye128_f32"])
    consts_done = pb.c("qc")
    sb = pb.ctx.enter_context
    XN1 = sb(nc.sbuf_tensor("XN1", [128, SL * 1024], BF16))
    WUP = sb(nc.sbuf_tensor("WUP", [128, 2 * 4096], BF16))
    BAND = sb(nc.sbuf_tensor("BAND", [128, 4224], BF16))
    LBS = sb(nc.sbuf_tensor("LBS", [32, 4096], BF16))
    LAS = sb(nc.sbuf_tensor("LAS", [128, SL * 32], BF16))
    GWS = sb(nc.sbuf_tensor("GWS", [128, SL * 4], BF16))
    EBS = sb(nc.sbuf_tensor("EBS", [1, 4], BF16))
    ZGT = sb(nc.sbuf_tensor("ZGT", [32, 8 * 128], BF16))
    X1SL = sb(nc.sbuf_tensor("X1SL", [128, 2 * 512], F32))
    OUTS = sb(nc.sbuf_tensor("OUTS", [128, 2 * 512], F32))
    T1C = sb(nc.sbuf_tensor("T1C", [128, 2 * 512], F32))
    GSC = sb(nc.sbuf_tensor("GSC", [128, 16], F32))
    ZGW = sb(nc.sbuf_tensor("ZGW", [128, 32], BF16))
    D0C = sb(nc.sbuf_tensor("D0C", [128, 1], F32))
    SSC = sb(nc.sbuf_tensor("SSC", [64, 16], F32))
    V12 = sb(nc.sbuf_tensor("V12", [64, 12], F32))
    LAM = sb(nc.sbuf_tensor("LAM", [64, 8], F32))
    BPS = sb(nc.sbuf_tensor("BPS", [64, 1], F32))
    VCS = sb(nc.sbuf_tensor("VCS", [64, 128], F32))
    KLP = sb(nc.sbuf_tensor("KLP", [128, 256], F32))
    ATRI = sb(nc.sbuf_tensor("ATRI", [128, 320], F32))
    SM64 = sb(nc.sbuf_tensor("SM64", [64, 6 * 64], F32))
    XT2 = sb(nc.sbuf_tensor("XT2", [64, 64], F32))
    Z256 = sb(nc.sbuf_tensor("Z256", [1, 256], BF16))
    PS = sb(nc.psum_tensor("PS", [128, 3584], F32))
    PT = sb(nc.psum_tensor("PT", [128, 512], BF16))

    KF = XN1[:, :].bitcast(F32)
    DIB = KF[0:64, 0:4096]
    DRB = KF[0:64, 4096:8192]
    MGB = KF[0:64, 8192:12288]
    IRB = KF[0:64, 12288:16384]
    IIB = WUP[:, :].bitcast(F32)[0:64, 0:4096]

    emit_phaseK(pb, nc, CS, consts_done, Lri, pri, qri, Ctri, Bp, VcrT, VciT,
                Dscl, lstep, arv, kpad, SSC, V12, LAM, BPS, VCS, KLP, ATRI,
                SM64, XT2, Z256, D0C, DIB, DRB, MGB, IRB, IIB, PS, PT, BAND)
    emit_l1_main(pb, nc, CS, xn1T, x1T, WupT, gwT, ebias, lATf, lBer, outT,
                 XN1, WUP, BAND, LBS, LAS, GWS, EBS, ZGT, X1SL, OUTS, T1C,
                 GSC, ZGW, D0C, PS, PT)
    pb.build()
    pb.ctx.close()
    return nc

def emit_phaseK(pb, nc, CS, consts_done, Lri, pri, qri, Ctri, Bp, VcrT, VciT,
                Dscl, lstep, arv, kpad, SSC, V12, LAM, BPS, VCS, KLP, ATRI,
                SM64, XT2, Z256, D0C, DIB, DRB, MGB, IRB, IIB, PS, PT, BAND,
                trow_dram=None, TRB=None):
    ld = pb.dma("y", "qk1", LAM[:, 0:2], Lri)
    pb.dma("y", "qk1", LAM[:, 2:4], pri)
    pb.dma("y", "qk1", LAM[:, 4:6], qri)
    pb.dma("y", "qk1", LAM[:, 6:8], Ctri)
    pb.dma("y", "qk1", BPS[:, :], Bp)
    pb.dma("y", "qk1", VCS[:, 0:64], VcrT)
    pb.dma("y", "qk1", VCS[:, 64:128], VciT)
    pb.dma("y", "qk1", SSC[0:1, 0:1], lstep)
    ld = pb.dma("y", "qk1", SSC[0:1, 1:2], Dscl)
    pb.wait("s", "qk1", ld)
    pb.wait("s", "qc", consts_done)
    st = pb.s("activation", SSC[0:1, 2:3], SSC[0:1, 0:1], AF.Exp)
    pb.wait("v", "s", st)
    rc = pb.v("reciprocal", SSC[0:1, 3:4], SSC[0:1, 2:3])
    pb.wait("v", "v", rc)
    s2 = pb.v("tensor_scalar_mul", SSC[0:1, 4:5], SSC[0:1, 3:4], 2.0)
    pb.wait("t", "v", s2)
    b1 = pb.tmm(PS[0:64, 0:1], CS["ones64_row_f32"][:, :], SSC[0:1, 4:5],
                start=True, stop=True)
    b2 = pb.tmm(PS[:, 512:513], CS["ones_row_f32"][:, :], SSC[0:1, 1:2],
                start=True, stop=True)
    b3 = pb.tmm(PS[0:64, 1024:1025], VCS[:, 0:64], BPS[:, :], start=True, stop=True)
    b4 = pb.tmm(PS[0:64, 1536:1537], VCS[:, 64:128], BPS[:, :], start=True, stop=True)
    pb.wait("v", "t", b4)
    e1 = pb.v("tensor_copy", SSC[:, 5:6], PS[0:64, 0:1])
    e2 = pb.v("tensor_copy", D0C[:, :], PS[:, 512:513])
    e3 = pb.v("tensor_copy", SSC[:, 6:7], PS[0:64, 1024:1025])
    e4 = pb.v("tensor_copy", SSC[:, 7:8], PS[0:64, 1536:1537])
    pb.wait("v", "v", e4)
    Ctr, Cti = LAM[:, 6:7], LAM[:, 7:8]
    pr_, pi_ = LAM[:, 2:3], LAM[:, 3:4]
    qr_, qi_ = LAM[:, 4:5], LAM[:, 5:6]
    Bcr, Bci = SSC[:, 6:7], SSC[:, 7:8]
    def cmul(dst_r, dst_i, ar, ai, br, bi, conj_a):
        pb.wait("v", "v", pb.c("v"))
        t1 = pb.v("tensor_mul", SSC[:, 8:9], ar, br)
        t2 = pb.v("tensor_mul", SSC[:, 9:10], ai, bi)
        pb.wait("v", "v", t2)
        if conj_a:
            rr = pb.v("tensor_add", dst_r, SSC[:, 8:9], SSC[:, 9:10])
        else:
            rr = pb.v("tensor_sub", dst_r, SSC[:, 8:9], SSC[:, 9:10])
        pb.wait("v", "v", rr)
        t3 = pb.v("tensor_mul", SSC[:, 8:9], ar, bi)
        t4 = pb.v("tensor_mul", SSC[:, 9:10], ai, br)
        pb.wait("v", "v", t4)
        if conj_a:
            r = pb.v("tensor_sub", dst_i, SSC[:, 8:9], SSC[:, 9:10])
        else:
            r = pb.v("tensor_add", dst_i, SSC[:, 8:9], SSC[:, 9:10])
        return r
    cmul(V12[:, 0:1], V12[:, 4:5], Ctr, Cti, Bcr, Bci, True)
    cmul(V12[:, 1:2], V12[:, 5:6], Ctr, Cti, pr_, pi_, True)
    cmul(V12[:, 2:3], V12[:, 6:7], qr_, qi_, Bcr, Bci, True)
    vv = cmul(V12[:, 3:4], V12[:, 7:8], qr_, qi_, pr_, pi_, True)
    pb.wait("v", "v", vv)
    vneg = pb.v("tensor_scalar_mul", V12[:, 8:12], V12[:, 4:8], -1.0)
    nlr = pb.v("tensor_scalar_mul", SSC[:, 10:11], LAM[:, 0:1], -1.0)
    prev_inv = None
    prev_tb = None
    for blk in range(8):
        cols = slice(blk * 512, (blk + 1) * 512)
        bcols = slice(0, 512)
        if prev_inv is not None:
            pb.wait("v", "v", prev_inv)
        if trow_dram is not None:
            if prev_tb is not None:
                pb.wait("y", "t", prev_tb)
            tl = pb.dma("y", "qtr", TRB, trow_dram[0:1, cols])
            pb.wait("t", "qtr", tl)
            trow_src = TRB
        else:
            trow_src = CS["t_row"][0:1, cols]
        pb.wait("t", "v", pb.c("v"))
        tb = pb.tmm(PS[0:64, 2048:2560], CS["ones64_row_f32"][:, :],
                    trow_src, start=True, stop=True)
        prev_tb = tb
        pb.wait("v", "t", tb)
        di = pb.v("tensor_scalar", DIB[:, bcols], PS[0:64, 2048:2560],
                  SSC[:, 5:6], LAM[:, 1:2], op0=ALU.mult, op1=ALU.subtract)
        dr = pb.v("tensor_scalar", DRB[:, bcols], PS[0:64, 2048:2560],
                  0.0, SSC[:, 10:11], op0=ALU.mult, op1=ALU.add)
        pb.wait("v", "v", dr)
        m1 = pb.v("tensor_mul", MGB[:, bcols], DRB[:, bcols], DRB[:, bcols])
        m1b = pb.v("tensor_mul", IRB[:, cols], DIB[:, bcols], DIB[:, bcols])
        pb.wait("v", "v", max(m1, m1b))
        m2 = pb.v("tensor_add", MGB[:, bcols], MGB[:, bcols], IRB[:, cols])
        pb.wait("v", "v", m2)
        m2b = pb.v("tensor_scalar_min", MGB[:, bcols], MGB[:, bcols], 3.0e38)
        pb.wait("v", "v", m2b)
        rcb = pb.v("reciprocal", MGB[:, bcols], MGB[:, bcols])
        pb.wait("v", "v", rcb)
        iv1 = pb.v("tensor_mul", IRB[:, cols], DRB[:, bcols], MGB[:, bcols])
        iv2 = pb.v("scalar_tensor_tensor", IIB[:, cols], DIB[:, bcols], -1.0,
                   MGB[:, bcols], op0=ALU.mult, op1=ALU.mult)
        prev_inv = iv2
    pb.wait("t", "v", prev_inv)
    ev_prev = None
    for j in range(SL):
        cols = slice(j * 128, (j + 1) * 128)
        if ev_prev is not None:
            pb.wait("t", "v", ev_prev)
        pb.mm(PS[:, 3072:3076], IRB[:, cols], V12[:, 0:4], start=True, stop=False)
        pb.mm(PS[:, 3072:3076], IIB[:, cols], V12[:, 8:12], start=False, stop=True)
        pb.mm(PS[:, 3076:3080], IIB[:, cols], V12[:, 0:4], start=True, stop=False)
        mmj = pb.tmm(PS[:, 3076:3080], IRB[:, cols], V12[:, 4:8], start=False, stop=True)
        pb.wait("v", "t", mmj)
        ev_prev = pb.v("tensor_copy",
                       KLP[:, :].rearrange("p (c a) -> p c a", a=32)[:, :, j:j + 1],
                       PS[:, 3072:3080].unsqueeze(2))
    def vs(method, *a, **kw):
        pb.wait("v", "v", pb.c("v"))
        return pb.v(method, *a, **kw)
    pb.wait("v", "v", ev_prev)
    k = lambda c0: KLP[:, c0 * 32:(c0 + 1) * 32]
    k00r, k01r, k10r, k11r = k(0), k(1), k(2), k(3)
    k00i, k01i, k10i, k11i = k(4), k(5), k(6), k(7)
    AR = lambda c0: ATRI[:, c0 * 32:(c0 + 1) * 32]
    denr, deni, magd = AR(0), AR(1), AR(2)
    qr2, qi2, s1_, s2_ = AR(3), AR(4), AR(5), AR(6)
    atr_r, atr_i = AR(7), AR(8)
    e = vs("tensor_scalar_add", denr, k11r, 1.0)
    vs("tensor_copy", deni, k11i)
    pb.wait("v", "v", pb.c("v"))
    e = vs("tensor_mul", magd, denr, denr)
    e = vs("tensor_mul", s1_, deni, deni)
    e = vs("tensor_add", magd, magd, s1_)
    e = vs("tensor_scalar_min", magd, magd, 3.0e38)
    e = vs("reciprocal", magd, magd)
    e1 = vs("tensor_mul", s1_, k01r, k10r)
    e2 = vs("tensor_mul", s2_, k01i, k10i)
    pb.wait("v", "v", e2)
    e3 = vs("tensor_sub", qr2, s1_, s2_)
    e1 = vs("tensor_mul", s1_, k01r, k10i)
    e2 = vs("tensor_mul", s2_, k01i, k10r)
    pb.wait("v", "v", e2)
    e4 = vs("tensor_add", qi2, s1_, s2_)
    pb.wait("v", "v", e4)
    e1 = vs("tensor_mul", s1_, qr2, denr)
    e2 = vs("tensor_mul", s2_, qi2, deni)
    pb.wait("v", "v", e2)
    e3 = vs("tensor_add", s1_, s1_, s2_)
    pb.wait("v", "v", e3)
    e3 = vs("tensor_mul", s1_, s1_, magd)
    e5 = vs("tensor_mul", s2_, qi2, denr)
    pb.wait("v", "v", e5)
    e6 = vs("scalar_tensor_tensor", s2_, qr2, -1.0, deni, op0=ALU.mult, op1=ALU.mult)
    pb.wait("v", "v", pb.c("v"))
    e7 = vs("tensor_mul", qr2, qr2, deni)
    e8 = vs("tensor_mul", qi2, qi2, denr)
    pb.wait("v", "v", max(e7, e8))
    e9 = vs("tensor_sub", s2_, qi2, qr2)
    pb.wait("v", "v", e9)
    e9 = vs("tensor_mul", s2_, s2_, magd)
    pb.wait("v", "v", e9)
    i1 = vs("tensor_sub", denr, k00r, s1_)
    i2 = vs("tensor_sub", deni, k00i, s2_)
    pb.wait("v", "v", max(i1, i2))
    tc = CS["t_col"]
    a1 = vs("tensor_mul", s1_, tc[:, :], deni)
    a2 = vs("tensor_mul", s2_, tc[:, :], denr)
    pb.wait("v", "v", max(a1, a2))
    a3 = vs("tensor_sub", atr_r, denr, s1_)
    a4 = vs("tensor_add", atr_i, deni, s2_)
    pb.wait("t", "v", max(a3, a4))
    tr1 = pb.op("t", "transpose", PS[0:32, 3072:3200], atr_r,
                CS["eye128_f32"][:, :], inc="t")
    tr2 = pb.op("t", "transpose", PS[0:32, 3200:3328], atr_i,
                CS["eye128_f32"][:, :], inc="t")
    pb.wait("v", "t", tr2)
    evt = pb.v("tensor_copy", ATRI[0:32, 0:256], PS[0:32, 3072:3328])
    pb.wait("y", "v", evt)
    s1d = pb.dma("y", "qk2", arv[0:1, :].rearrange("o (a w) -> o a w", w=128),
                 ATRI[0:32, 0:128].unsqueeze(1))
    s2d = pb.dma("y", "qk2", arv[1:2, :].rearrange("o (a w) -> o a w", w=128),
                 ATRI[0:32, 128:256].unsqueeze(1))
    pb.wait("y", "qk2", s2d)
    l1 = pb.dma("y", "qk2", SM64[:, 0:64], arv[0:1, :].rearrange("o (a w) -> (o a) w", w=64))
    l2 = pb.dma("y", "qk2", SM64[:, 64:128], arv[1:2, :].rearrange("o (a w) -> (o a) w", w=64))
    pb.wait("t", "qk2", l2)
    ARE, AIM = SM64[:, 0:64], SM64[:, 64:128]
    pb.mm(PS[0:64, 3072:3136], ARE, CS["E64r"][:, :], start=True, stop=False)
    g1 = pb.tmm(PS[0:64, 3072:3136], AIM, CS["E64i_neg"][:, :], start=False, stop=True)
    pb.mm(PS[0:64, 3136:3200], ARE, CS["E64i"][:, :], start=True, stop=False)
    g2 = pb.tmm(PS[0:64, 3136:3200], AIM, CS["E64r"][:, :], start=False, stop=True)
    pb.wait("v", "t", g2)
    gv = pb.v("tensor_copy", SM64[0:64, 128:256], PS[0:64, 3072:3200])
    pb.wait("v", "v", gv)
    GRE, GIM = SM64[:, 128:192], SM64[:, 192:256]
    HRE, HIM = SM64[:, 256:320], SM64[:, 320:384]
    t1 = pb.v("tensor_mul", HRE, GRE, CS["TWr"][:, :])
    t2 = pb.v("tensor_mul", XT2[:, :], GIM, CS["TWi"][:, :])
    pb.wait("v", "v", max(t1, t2))
    t3 = pb.v("tensor_sub", HRE, HRE, XT2[:, :])
    t4 = pb.v("tensor_mul", HIM, GIM, CS["TWr"][:, :])
    pb.wait("v", "v", max(t3, t4))
    t6 = pb.v("tensor_mul", XT2[:, :], GRE, CS["TWi"][:, :])
    pb.wait("v", "v", t6)
    t7 = pb.v("tensor_add", HIM, HIM, XT2[:, :])
    pb.wait("t", "v", t7)
    pb.mm(PS[0:64, 3072:3136], HRE, CS["EPr"][:, :], start=True, stop=False)
    x2 = pb.tmm(PS[0:64, 3072:3136], HIM, CS["EPi_neg"][:, :], start=False, stop=True)
    pb.wait("v", "t", x2)
    xv = pb.v("tensor_copy", XT2[:, :], PS[0:64, 3072:3136])
    pb.wait("t", "v", xv)
    xt = pb.op("t", "transpose", PS[0:64, 3136:3200], XT2[:, :],
               CS["eye64_f32"][0:64, :], inc="t")
    pb.wait("v", "t", xt)
    xv2 = pb.v("tensor_copy", XT2[:, :], PS[0:64, 3136:3200])
    zm = pb.v("memset", Z256[:, :], 0.0)
    pb.wait("g", "v", max(xv2, zm))
    z1 = pb.dma("g", "qk3", kpad[0:256].unsqueeze(0), Z256[:, :])
    k1_ = pb.dma("g", "qk3", kpad[256:4352].rearrange("(a w) -> a w", w=64), XT2[:, :])
    pb.wait("g", "qk3", k1_)
    pb.wait("y", "qk3", k1_)
    pb.wait("s", "qk3", k1_)
    for p128 in range(128):
        q = ("y", "s")[p128 % 2]
        pb.dma(q, "qbd" if q == "y" else "qbd2", BAND[p128:p128 + 1, :],
               kpad[128 - p128: 128 - p128 + 4224].unsqueeze(0))
    pb.kband_done = pb.c("qbd")

def emit_l1_main(pb, nc, CS, xn1T, x1T, WupT, gwT, ebias, lATf, lBer, outT,
                 XN1, WUP, BAND, LBS, LAS, GWS, EBS, ZGT, X1SL, OUTS, T1C,
                 GSC, ZGW, D0C, PS, PT):
    xn1_done = getattr(pb, "xn1_slab_done", None)
    pb.wait("g", "t", pb.c("t"))
    pb.wait("g", "v", pb.c("v") if xn1_done is None else xn1_done[0])
    ll = pb.dma("g", "qlw", LAS[:, :].rearrange("p (a w) -> p a w", w=32),
                lATf.rearrange("(a p) w -> p a w", p=128))
    gl = pb.dma("g", "qlw", GWS[:, :].rearrange("p (a w) -> p a w", w=4),
                gwT.rearrange("(a p) w -> p a w", p=128))
    el = pb.dma("g", "qlw", EBS[:, :].unsqueeze(1), ebias.unsqueeze(1))
    bl = pb.dma("g", "qlw", LBS[:, :].rearrange("p (a w) -> p a w", w=128),
                lBer.rearrange("p (a w) -> p a w", w=128))
    prev_zg = None
    prev_tr = None
    for tt in range(8):
        pb.wait("t", "qlw", bl)
        if prev_zg is not None:
            pb.wait("t", "v", prev_zg)
        for f in range(SL):
            if tt == 0 and xn1_done is not None:
                pb.wait("t", "v", xn1_done[f])
            pb.mm(PS[:, 0:4], XN1[:, f * 1024 + tt * 128: f * 1024 + (tt + 1) * 128],
                  GWS[:, f * 4:(f + 1) * 4], start=(f == 0), stop=False)
        lmm = pb.tmm(PS[:, 0:4], CS["ones_row_bf"][:, :], EBS[:, :],
                     start=False, stop=True)
        for f in range(SL):
            fn = pb.tmm if f == SL - 1 else pb.mm
            zmm = fn(PS[:, 512:544], XN1[:, f * 1024 + tt * 128: f * 1024 + (tt + 1) * 128],
                     LAS[:, f * 32:(f + 1) * 32], start=(f == 0), stop=(f == SL - 1))
        pb.wait("v", "t", zmm)
        lg = GSC[:, 0:4]
        ev = pb.v("tensor_copy", lg, PS[:, 0:4])
        pb.wait("v", "v", ev)
        mx = pb.v("tensor_reduce", GSC[:, 4:5], lg, axis=AX.X, op=ALU.max)
        pb.wait("v", "v", mx)
        ge = pb.v("tensor_single_scalar", GSC[:, 5:9], lg, GSC[:, 4:5], op=ALU.is_ge)
        pb.wait("v", "v", ge)
        oh = GSC[:, 12:16]
        e1 = pb.v("tensor_copy", oh[:, 0:1], GSC[:, 5:6])
        n0 = pb.v("tensor_scalar", GSC[:, 9:10], GSC[:, 5:6], -1.0, 1.0,
                  op0=ALU.mult, op1=ALU.add)
        pb.wait("v", "v", n0)
        g1 = pb.v("tensor_mul", oh[:, 1:2], GSC[:, 6:7], GSC[:, 9:10])
        pb.wait("v", "v", g1)
        t1 = pb.v("tensor_scalar", GSC[:, 10:11], oh[:, 1:2], -1.0, 1.0,
                  op0=ALU.mult, op1=ALU.add)
        pb.wait("v", "v", t1)
        n1 = pb.v("tensor_mul", GSC[:, 10:11], GSC[:, 9:10], GSC[:, 10:11])
        pb.wait("v", "v", n1)
        g2 = pb.v("tensor_mul", oh[:, 2:3], GSC[:, 7:8], GSC[:, 10:11])
        pb.wait("v", "v", g2)
        t2 = pb.v("tensor_scalar", GSC[:, 11:12], oh[:, 2:3], -1.0, 1.0,
                  op0=ALU.mult, op1=ALU.add)
        pb.wait("v", "v", t2)
        n2 = pb.v("tensor_mul", GSC[:, 11:12], GSC[:, 10:11], GSC[:, 11:12])
        pb.wait("v", "v", n2)
        g3 = pb.v("tensor_mul", oh[:, 3:4], GSC[:, 8:9], GSC[:, 11:12])
        pb.wait("v", "v", g3)
        prev_zg = zg = pb.v("tensor_tensor",
                            ZGW[:, :].rearrange("p (e r) -> p e r", e=4),
                            PS[:, 512:544].rearrange("p (e r) -> p e r", e=4),
                            oh.unsqueeze(2).broadcast_to([128, 4, 8]), op=ALU.mult)
        pb.wait("t", "v", zg)
        if prev_tr is not None:
            pb.wait("t", "v", prev_tr)
        trz = pb.op("t", "transpose", PT[0:32, 0:128], ZGW[:, :], CS["eye"][:, :], inc="t")
        pb.wait("v", "t", trz)
        prev_tr = pb.v("tensor_copy", ZGT[:, tt * 128:(tt + 1) * 128], PT[0:32, 0:128])
    zgt_done = prev_tr
    wu_cnt = [None, None]
    momm = [None, None]
    evd = [None, None]
    std = [None, None]
    xld = [None, None]
    t_entry = pb.c("t")
    for od in range(SL):
        par = od % 2
        pb.wait("g", "t", momm[par] if momm[par] is not None else t_entry)
        wu_cnt[par] = pb.dma("g", "qwu%d" % par, _sb3(WUP, par), _wpanel_ap(WupT, od))
        if xld[par] is not None:
            pb.wait("y", "v", evd[par])
        elif xn1_done is not None:
            pb.wait("y", "v", xn1_done[SL - 1])
        xld[par] = pb.dma("y", "qx1r%d" % par,
                          X1SL[:, par * 512:(par + 1) * 512].unsqueeze(1),
                          x1T[od * 128:(od + 1) * 128, 0:512].unsqueeze(1))
        xl2 = pb.dma("y", "qx1r%d" % par,
                     T1C[:, par * 512:(par + 1) * 512].unsqueeze(1),
                     x1T[od * 128:(od + 1) * 128, 512:1024].unsqueeze(1))
        pb.wait("t", "qwu%d" % par, wu_cnt[par])
        pb.wait("t", "qbd", pb.c("qbd"))
        pb.wait("t", "qbd2", pb.c("qbd2"))
        if od == 0:
            pb.wait("t", "v", zgt_done)
        if evd[par] is not None:
            pb.wait("t", "v", evd[par])
        for hf in range(2):
            ps = PS[:, 1024 + par * 1024 + hf * 512: 1024 + par * 1024 + (hf + 1) * 512]
            for f in range(SL):
                if od == 0 and hf == 0 and xn1_done is not None:
                    pb.wait("t", "v", xn1_done[f])
                pb.mm(ps, WUP[:, par * 4096 + f * 128: par * 4096 + (f + 1) * 128],
                      XN1[:, f * 1024 + hf * 512: f * 1024 + (hf + 1) * 512],
                      start=(f == 0), stop=False)
            for mb in range(od + 1):
                pb.mm(ps, BAND[:, (od - mb + 1) * 128: (od - mb + 2) * 128],
                      XN1[:, mb * 1024 + hf * 512: mb * 1024 + (hf + 1) * 512],
                      start=False, stop=False)
            r = pb.tmm(ps, LBS[:, od * 128:(od + 1) * 128],
                       ZGT[0:32, hf * 512:(hf + 1) * 512],
                       start=False, stop=True)
        momm[par] = r
        pb.wait("v", "t", momm[par])
        pb.wait("v", "qx1r%d" % par, xl2)
        if std[par] is not None:
            pb.wait("v", "qo%d" % par, std[par])
        ee = []
        for hf in range(2):
            ps = PS[:, 1024 + par * 1024 + hf * 512: 1024 + par * 1024 + (hf + 1) * 512]
            xn1sl = XN1[:, od * 1024 + hf * 512: od * 1024 + (hf + 1) * 512]
            a = pb.v("scalar_tensor_tensor", OUTS[:, par * 512:(par + 1) * 512],
                     xn1sl, D0C[:, :], ps, op0=ALU.mult, op1=ALU.add)
            pb.wait("v", "v", a)
            x1s = X1SL[:, par * 512:(par + 1) * 512] if hf == 0 else T1C[:, par * 512:(par + 1) * 512]
            b = pb.v("tensor_add", OUTS[:, par * 512:(par + 1) * 512],
                     OUTS[:, par * 512:(par + 1) * 512], x1s)
            pb.wait("y", "v", b)
            ee.append(b)
            std[par] = pb.dma("y", "qo%d" % par,
                              outT[od * 128:(od + 1) * 128, hf * 512:(hf + 1) * 512],
                              OUTS[:, par * 512:(par + 1) * 512])
            if hf == 0:
                pb.wait("v", "qo%d" % par, std[par])
        evd[par] = ee[-1]

def _phase_barrier(pb):
    names = list(pb.cnt.keys())
    for st in ("t", "v", "s", "g", "y"):
        for n in names:
            if pb.cnt[n] > 0:
                pb.wait(st, n, pb.cnt[n])

def build_fused(sim_local_gather=False):
    nc = bass.Bass("TRN2", target_bir_lowering=False, debug=False)
    C = _consts()
    xT = nc.dram_tensor("xT", [D, TOK], F32, kind="ExternalInput").ap()
    Wq = nc.dram_tensor("Wq", [D, D], F32, kind="ExternalInput").ap()
    Wkv = nc.dram_tensor("Wkv", [D, KVD], F32, kind="ExternalInput").ap()
    WrT = nc.dram_tensor("WrT", [D, 64], F32, kind="ExternalInput").ap()
    ATf = nc.dram_tensor("ATf", [D, 1024], F32, kind="ExternalInput").ap()
    B0kr = nc.dram_tensor("B0kr", [1024, D], F32, kind="ExternalInput").ap()
    Wo = nc.dram_tensor("Wo", [D, D], F32, kind="ExternalInput").ap()
    bo = nc.dram_tensor("bo", [128, 32], F32, kind="ExternalInput").ap()
    masks = nc.dram_tensor("masks", [64, 128, 128], BF16, kind="ExternalInput").ap()
    WupT = nc.dram_tensor("WupT", [D, D], F32, kind="ExternalInput").ap()
    gwT = nc.dram_tensor("gwT", [D, 4], F32, kind="ExternalInput").ap()
    ebias = nc.dram_tensor("ebias", [1, 4], F32, kind="ExternalInput").ap()
    lATf = nc.dram_tensor("lATf", [D, 32], F32, kind="ExternalInput").ap()
    lBer = nc.dram_tensor("lBer", [32, D], F32, kind="ExternalInput").ap()
    Lri = nc.dram_tensor("Lri", [64, 2], F32, kind="ExternalInput").ap()
    pri = nc.dram_tensor("pri", [64, 2], F32, kind="ExternalInput").ap()
    qri = nc.dram_tensor("qri", [64, 2], F32, kind="ExternalInput").ap()
    Ctri = nc.dram_tensor("Ctri", [64, 2], F32, kind="ExternalInput").ap()
    Bp = nc.dram_tensor("Bp", [64, 1], F32, kind="ExternalInput").ap()
    VcrT = nc.dram_tensor("VcrT", [64, 64], F32, kind="ExternalInput").ap()
    VciT = nc.dram_tensor("VciT", [64, 64], F32, kind="ExternalInput").ap()
    Dscl = nc.dram_tensor("Dscl", [1, 1], F32, kind="ExternalInput").ap()
    lstep = nc.dram_tensor("lstep", [1, 1], F32, kind="ExternalInput").ap()
    outT = nc.dram_tensor("outT", [D, TOK], F32, kind="ExternalOutput").ap()
    kvx = nc.dram_tensor("kvx", [KVD, TOK], BF16).ap()
    kvg = nc.dram_tensor("kvg", [8 * KVD, TOK], BF16).ap()
    qTd = nc.dram_tensor("qTd", [D, TOK], BF16).ap()
    moeTd = nc.dram_tensor("moeTd", [D, TOK], F32).ap()
    x1Td = nc.dram_tensor("x1Td", [D, TOK], F32).ap()
    xn1Td = nc.dram_tensor("xn1Td", [D, TOK], BF16).ap()
    rbounce = nc.dram_tensor("rbounce", [32, 512], F32).ap()
    arv = nc.dram_tensor("arv", [2, 4096], F32).ap()
    kpad = nc.dram_tensor("kpad", [4352], BF16).ap()

    pb = PB(nc)
    CS = _load_consts(pb, nc, C, ["eye", "ones_col", "ones_row_f32", "basis32",
                                  "t_col", "E64r", "E64i", "E64i_neg",
                                  "TWr", "TWi", "EPr", "EPi_neg", "eye64_f32",
                                  "ones_row_bf", "ones64_row_f32", "eye128_f32"])
    trow_dram = nc.inline_tensor(np.ascontiguousarray(C["t_row"]), name="i_t_row").ap()
    sb = pb.ctx.enter_context
    BOC = sb(nc.sbuf_tensor("BOC", [128, 32], F32))
    pb.dma("y", "qc", BOC[:, :], bo)
    consts_done = pb.c("qc")

    BIG1 = sb(nc.sbuf_tensor("BIG1", [128, 32768], BF16))
    BIG2 = sb(nc.sbuf_tensor("BIG2", [128, 32768], BF16))
    B2F = BIG2[:, :].bitcast(F32)
    BIG3 = sb(nc.sbuf_tensor("BIG3", [128, 22016], BF16))
    WPAN = sb(nc.sbuf_tensor("WPAN", [128, 2 * 4096], BF16))
    MBC = sb(nc.sbuf_tensor("MBC", [128, 1024], F32))
    RBC = sb(nc.sbuf_tensor("RBC", [128, 1024], F32))
    WRTS = sb(nc.sbuf_tensor("WRTS", [128, SL * 64], BF16))
    PS = sb(nc.psum_tensor("PS", [128, 3584], F32))
    PT = sb(nc.psum_tensor("PT", [128, 1024], BF16))
    PTF = PT[:, :].bitcast(F32)

    B3F = BIG3[:, :].bitcast(F32)

    XN = BIG1[:, :]
    AT = BIG2[:, :]
    XSL = B3F[:, 0:2048]
    TMP = B3F[:, 2048:3072]
    STS = B3F[0:1, 3072:5120]
    ROWS = B3F[0:1, 5120:8192]
    KOUT = BIG3[:, 0:1024]
    VTT = BIG3[:, 1024:2048]
    VOUT = BIG3[:, 2048:3072]
    QOUT = BIG3[:, 3072:5120]
    SC = B3F[:, 2560:3072]
    GTMP = B3F[:, 3072:3584]
    GTMP2 = B3F[:, 3584:4096]
    MOUT = B3F[:, 4096:5120]
    GT = BIG3[:, 10240:10752]
    HGtmp = BIG3[:, 10752:11776]
    B0P = BIG3[:, 11776:13824]
    HGT = BIG3[:, 13824:22016]

    emit_ln0(pb, nc, CS, xT, XN, XSL, MBC, RBC, TMP, STS, ROWS, PS, consts_done)
    pb.dma("g", "qr", WRTS[:, :].rearrange("p (a w) -> p a w", w=64),
           WrT.rearrange("(a p) w -> p a w", p=128))
    wrt_done = pb.cnt["qr"]
    def at_load(f):
        pb.dma("g", "qa", AT[:, f * 1024:(f + 1) * 1024].rearrange("p (a w) -> p a w", a=1),
               ATf[f * 128:(f + 1) * 128, :].unsqueeze(1))
    emit_kv(pb, nc, CS, Wkv, XN, WPAN, KOUT, VTT, VOUT, kvx, PS, PT,
            bg_loads=[(lambda f=f: at_load(f)) for f in range(16)])
    pb.wait("g", "qk", pb.c("qk"))
    pb.wait("g", "qvs", pb.c("qvs"))
    if sim_local_gather:
        for rr in range(NC):
            pb.dma("g", "qcc", kvg[rr * KVD:(rr + 1) * KVD, :], kvx[:, :])
        pb.op("g", "nop", inc="cc")
    else:
        pb.op("g", "collective_compute", "AllGather", ALU.bypass,
              replica_groups=[list(range(NC))], ins=[kvx.opt()], outs=[kvg.opt()],
              inc="cc")
    emit_q(pb, nc, Wq, XN, WPAN, QOUT, qTd, PS,
           bg_loads=[(lambda f=f: at_load(f)) for f in range(16, SL)])
    at_done = pb.cnt["qa"]
    emit_moe(pb, nc, CS, XN, AT, at_done, WRTS, wrt_done, B0kr, B0P,
             SC, GT, GTMP, GTMP2, HGtmp, HGT, MOUT, moeTd, PS, PT)

    _phase_barrier(pb)
    QP = BIG2[:, 0:16384]
    CTX = BIG1[:, :].bitcast(F32)
    KB = BIG2[:, 16384:24576]
    VB = BIG2[:, 24576:32768]
    WOP = WPAN[:, :]
    PSB = BIG3[:, 0:2048]
    MKB = BIG3[:, 2048:3072]
    RCP = B3F[0:32, 1536:2048]
    RB = B3F[:, 2048:2560]
    XSL2 = B3F[:, 2560:3584]
    MSL = B3F[:, 3584:4608]
    T1 = B3F[:, 4608:5120]
    X1O = B3F[:, 5120:6144]
    X1B = BIG3[:, 12288:12800]
    SQB = BIG3[:, 12800:13312]
    STS2 = B3F[0:1, 6656:8704]
    ROW2 = B3F[0:1, 8704:10752]
    MB2 = MBC[:, :]
    RB2 = RBC[:, :]
    RCROW = WRTS[:, :].bitcast(F32)[0:1, 0:1024]
    RCROW8 = B2F[0:1, 8192:16384]

    pb.wait("y", "cc", 1)
    x1st = emit_attention(pb, nc, CS, consts_done, qTd, kvg, xT, moeTd, Wo, masks,
                          QP, CTX, KB, VB, PSB, MKB, WOP, RCP, RCROW, rbounce,
                          RB, BOC, XSL2, MSL, T1, X1O, X1B, SQB, STS2, PS, x1Td,
                          PTF[0:1, 0:512], RCROW8=RCROW8)

    _phase_barrier(pb)
    XN1 = BIG1[:, :]
    KF = BIG2[:, :].bitcast(F32)
    IRB = KF[0:64, 0:4096]
    IIB = KF[0:64, 4096:8192]
    DIB = KF[0:64, 8192:8704]
    DRB = KF[0:64, 8704:9216]
    MGB = KF[0:64, 9216:9728]
    WUP = BIG2[:, 0:8192]
    BAND = BIG2[:, 8192:12416]
    LBS = BIG2[0:32, 12416:16512]
    LAS = BIG2[:, 16512:17536]
    GWS = BIG2[:, 17536:17664]
    EBS = BIG2[0:1, 17664:17668]
    ZGT = BIG2[0:32, 17668:18692]
    ZGW = BIG2[:, 18692:18724]
    X1SL = BIG2[:, 20480:22528].bitcast(F32)
    OUTS = BIG2[:, 22528:24576].bitcast(F32)
    T1C = BIG2[:, 24576:26624].bitcast(F32)
    GSC = B2F[:, 13312:13328]
    D0C = B2F[:, 13328:13329]
    SSC = B2F[0:64, 13330:13346]
    V12 = B2F[0:64, 13346:13358]
    LAM = B2F[0:64, 13358:13366]
    BPS = B2F[0:64, 13366:13367]
    VCS = B2F[0:64, 13368:13496]
    KLP = B2F[:, 13496:13752]
    ATRI = B2F[:, 13752:14072]
    SM64 = B2F[0:64, 14072:14456]
    XT2 = B2F[0:64, 14456:14520]
    TRB = B2F[0:1, 14520:15032]
    Z256 = BIG2[0:1, 30720:30976]
    emit_phaseK(pb, nc, CS, consts_done, Lri, pri, qri, Ctri, Bp, VcrT, VciT,
                Dscl, lstep, arv, kpad, SSC, V12, LAM, BPS, VCS, KLP, ATRI,
                SM64, XT2, Z256, D0C, DIB, DRB, MGB, IRB, IIB, PS, PT, BAND,
                trow_dram, TRB)
    XR = BIG2[:, 19456:23552].bitcast(F32)
    emit_xn1(pb, nc, CS, x1Td, xn1Td, STS2, ROW2, MB2, RB2, XR, XN1, PS, x1st,
             VROW=MBC[0:1, 0:1024])
    emit_l1_main(pb, nc, CS, xn1Td, x1Td, WupT, gwT, ebias, lATf, lBer, outT,
                 XN1, WUP, BAND, LBS, LAS, GWS, EBS, ZGT, X1SL, OUTS, T1C,
                 GSC, ZGW, D0C, PS, PT)
    pb.build()
    pb.ctx.close()
    return nc
